# version 21
# baseline (speedup 1.0000x reference)
import contextlib
import math

import numpy as np
import ml_dtypes

import concourse.bass as bass
import concourse.mybir as mybir
from concourse.bass_utils import run_bass_kernel_spmd

F32 = mybir.dt.float32
BF16 = mybir.dt.bfloat16
AF = mybir.ActivationFunctionType
ALU = mybir.AluOpType

B, S, D, H, DH = 4, 4096, 2048, 16, 128
FF, E, R, ER = 8192, 8, 8, 64
NCH = 16
NFC = 64
NSLOT = 4
TOWN = 2048
T = 512
EPS = 1e-5
ISQ = 1.0 / math.sqrt(DH)
NEG = -1.0e30
W1XC = D + 192
W2XC = FF + 128
W2HA = 4096
W2HB = W2XC - W2HA
ENGINES = ("sync", "gpsimd", "tensor", "vector", "scalar")


class Sem:
    def __init__(self, ctx, nc, name):
        self.h = ctx.enter_context(nc.semaphore(name))
        self.n = 0


class Prog:

    def __init__(self):
        self.ops = {e: [] for e in ENGINES}

    def add(self, engine, fn=None, waits=(), inc=None, k=1):
        w = [(s, v) for (s, v) in waits if v > 0]
        self.ops[engine].append((w, fn, inc, k))
        if inc is not None:
            inc.n += k
            return inc.n
        return None

    def play(self, blk):
        for name in ENGINES:
            ops = self.ops[name]
            if not ops:
                continue

            def run(eng, ops=ops):
                for waits, fn, inc, k in ops:
                    for (sem, val) in waits:
                        eng.wait_ge(sem.h, val)
                    instr = fn(eng) if fn is not None else None
                    if inc is not None:
                        if instr is None:
                            instr = eng.nop()
                        instr.then_inc(inc.h, k)
            getattr(blk, name)(run)


def build_program(phases=("p1a", "p1b", "p1c", "p2", "p3", "p4"),
                  debug_spill=False):
    nc = bass.Bass("TRN2", target_bir_lowering=False)
    dt_spill = "ExternalOutput" if debug_spill else "Internal"

    xf = nc.dram_tensor("xf", [D, S], F32, kind="ExternalInput").ap()
    xo = nc.dram_tensor("xo", [D, TOWN], F32, kind="ExternalInput").ap()
    wq = nc.dram_tensor("wq", [128, NCH * D], F32, kind="ExternalInput").ap()
    wk = nc.dram_tensor("wk", [128, NCH * D], F32, kind="ExternalInput").ap()
    wv = nc.dram_tensor("wv", [128, NCH * D], F32, kind="ExternalInput").ap()
    wo = nc.dram_tensor("wo", [128, NCH * D], F32, kind="ExternalInput").ap()
    w1x = nc.dram_tensor("w1x", [NFC, 128, W1XC], F32, kind="ExternalInput").ap()
    w2x = nc.dram_tensor("w2x", [NCH, 128, W2XC], F32, kind="ExternalInput").ap()
    a1r = nc.dram_tensor("a1r", [128, NCH * ER], F32, kind="ExternalInput").ap()
    wrr = nc.dram_tensor("wrr", [128, NCH * E], F32, kind="ExternalInput").ap()
    brc = nc.dram_tensor("brc", [E, 1], F32, kind="ExternalInput").ap()
    b1cd = nc.dram_tensor("b1cd", [128, NFC], F32, kind="ExternalInput").ap()
    b2rowd = nc.dram_tensor("b2rowd", [1, D], F32, kind="ExternalInput").ap()
    cosf = nc.dram_tensor("cosf", [128, S], F32, kind="ExternalInput").ap()
    sinf = nc.dram_tensor("sinf", [128, S], F32, kind="ExternalInput").ap()
    coso = nc.dram_tensor("coso", [128, TOWN], F32, kind="ExternalInput").ap()
    sino = nc.dram_tensor("sino", [128, TOWN], F32, kind="ExternalInput").ap()
    msk8 = nc.dram_tensor("msk8", [128, 8 * T], F32, kind="ExternalInput").ap()
    x864d = nc.dram_tensor("x864d", [8, ER], F32, kind="ExternalInput").ap()

    q_sp = nc.dram_tensor("q_sp", [H, 128, TOWN], BF16, kind=dt_spill).ap()
    k_sp = nc.dram_tensor("k_sp", [H, 128, S], BF16, kind=dt_spill).ap()
    v_sp = nc.dram_tensor("v_sp", [H, 128, S], BF16, kind=dt_spill).ap()
    xn1_sp = nc.dram_tensor("xn1_sp", [NCH, 128, S], BF16, kind=dt_spill).ap()
    x1_sp = nc.dram_tensor("x1_sp", [NCH, 128, TOWN], F32, kind=dt_spill).ap()
    xn2_sp = nc.dram_tensor("xn2_sp", [NCH, 128, TOWN], BF16, kind=dt_spill).ap()
    w1cache = nc.dram_tensor("w1cache", [NFC, 128, W1XC], BF16, kind="Internal").ap()
    w2cache = nc.dram_tensor("w2cache", [NCH, 128, W2XC], BF16, kind="Internal").ap()
    attn_dbg = None
    srow_dbg = None
    if debug_spill:
        attn_dbg = nc.dram_tensor("attn_dbg", [128, H * TOWN], BF16,
                                  kind="ExternalOutput").ap()
        srow_dbg = nc.dram_tensor("srow_dbg", [1, 4 * T], F32,
                                  kind="ExternalOutput").ap()
    out = nc.dram_tensor("out", [D, TOWN], F32, kind="ExternalOutput").ap()

    with contextlib.ExitStack() as ctx:
        ec = ctx.enter_context
        ones_bf = ec(nc.sbuf_tensor("ones_bf", [128, 1], BF16))
        onesr_f = ec(nc.sbuf_tensor("onesr_f", [1, 128], F32))
        ones8_f = ec(nc.sbuf_tensor("ones8_f", [8, 1], F32))
        x864 = ec(nc.sbuf_tensor("x864", [8, ER], F32))


        bar = Sem(ctx, nc, "bar")
        ldx = Sem(ctx, nc, "ldx")
        ldw = Sem(ctx, nc, "ldw")
        stq = Sem(ctx, nc, "stq")
        pe = Sem(ctx, nc, "pe")
        ve = Sem(ctx, nc, "ve")
        se = Sem(ctx, nc, "se")
        po = Sem(ctx, nc, "po")
        lds = Sem(ctx, nc, "lds")

        blk = ec(nc.Block())

        def barrier(p):
            p.add("sync", lambda e: e.nop(),
                  waits=[(ldx, ldx.n), (stq, stq.n)], inc=bar)
            p.add("gpsimd", lambda e: e.nop(), waits=[(ldw, ldw.n)], inc=bar)
            p.add("tensor", lambda e: e.nop(), inc=bar)
            p.add("vector", lambda e: e.nop(), inc=bar)
            p.add("scalar", lambda e: e.nop(), inc=bar)
            t = bar.n
            for en in ENGINES:
                p.add(en, None, waits=[(bar, t)])

        def ln_rows(p, srow, psA, psB, n, stats_tok):
            def rows1(e):
                e.mul(srow[0:1, 0:T], psA[0:1, 0:T], 1.0 / n)
                e.activation(srow[0:1, 2 * T:3 * T], srow[0:1, 0:T], AF.Square)
                return e.mul(srow[0:1, 3 * T:4 * T], psB[0:1, 0:T], 1.0 / n)
            p.add("scalar", rows1, waits=[(pe, stats_tok)], inc=se)
            pretok = se.n
            p.add("vector",
                  lambda e: e.scalar_tensor_tensor(
                      srow[0:1, 2 * T:3 * T], srow[0:1, 3 * T:4 * T], EPS,
                      srow[0:1, 2 * T:3 * T], ALU.add, ALU.subtract),
                  waits=[(se, pretok)], inc=ve)
            vartok = ve.n

            def rows2(e):
                e.activation(srow[0:1, 3 * T:4 * T], srow[0:1, 2 * T:3 * T],
                             AF.Ln)
                return e.activation(srow[0:1, T:2 * T], srow[0:1, 3 * T:4 * T],
                                    AF.Exp, scale=-0.5)
            p.add("scalar", rows2, waits=[(ve, vartok)], inc=se)
            return se.n, pretok

        p = Prog()

        def _consts(e):
            e.memset(ones_bf[:], 1.0)
            e.memset(onesr_f[:], 1.0)
            return e.memset(ones8_f[:], 1.0)
        p.add("vector", _consts, inc=ve)
        p.add("sync", lambda e: e.dma_start(out=x864[:], in_=x864d), inc=ldx, k=16)
        barrier(p)
        p.play(blk)

        SWAP = [i ^ 1 for i in range(32)]

        def qk_phase(name, w_dram, xsrc, tok0s, cos_d, sin_d, o_sp, spill_xn,
                     tbl_cols):
            ph = contextlib.ExitStack()
            w_sb = ph.enter_context(nc.sbuf_tensor(name + "w", [128, NCH * D], BF16))
            xbf = [ph.enter_context(nc.sbuf_tensor(f"{name}xb{i}", [128, NCH * T], BF16)) for i in range(2)]
            xn = [ph.enter_context(nc.sbuf_tensor(f"{name}xn{i}", [128, NCH * T], BF16)) for i in range(2)]
            srow = ph.enter_context(nc.sbuf_tensor(name + "sr", [1, 4 * T], F32))
            mbc = ph.enter_context(nc.sbuf_tensor(name + "mb", [128, T], F32))
            rbc = ph.enter_context(nc.sbuf_tensor(name + "rb", [128, T], F32))
            cosb = ph.enter_context(nc.sbuf_tensor(name + "co", [128, tbl_cols], F32))
            sinb = ph.enter_context(nc.sbuf_tensor(name + "si", [128, tbl_cols], F32))
            qf = [ph.enter_context(nc.sbuf_tensor(f"{name}qf{i}", [128, T], F32)) for i in range(2)]
            qs = [ph.enter_context(nc.sbuf_tensor(f"{name}qs{i}", [128, T], F32)) for i in range(2)]
            qm = [ph.enter_context(nc.sbuf_tensor(f"{name}qm{i}", [128, T], F32)) for i in range(2)]
            qo = [ph.enter_context(nc.sbuf_tensor(f"{name}qo{i}", [128, T], BF16)) for i in range(2)]
            tmp = ph.enter_context(nc.sbuf_tensor(name + "tp", [128, T], F32))
            psq = [ph.enter_context(nc.psum_tensor(f"{name}pq{i}", [128, T], F32)) for i in range(2)]
            psA = ph.enter_context(nc.psum_tensor(name + "pA", [128, T], F32))
            psB = ph.enter_context(nc.psum_tensor(name + "pB", [128, T], F32))
            pbc = ph.enter_context(nc.psum_tensor(name + "pC", [128, T], F32))

            p = Prog()
            p.add("gpsimd", lambda e: e.dma_start(out=w_sb[:], in_=w_dram),
                  inc=ldw, k=16)
            wtok = ldw.n
            p.add("sync", lambda e: e.dma_start(out=cosb[:], in_=cos_d), inc=ldx, k=16)
            p.add("sync", lambda e: e.dma_start(out=sinb[:], in_=sin_d), inc=ldx, k=16)
            tbltok = ldx.n

            end_tok, rope_ve, rope_st, qf_se, norm_ve, stats_pe = {}, {}, {}, {}, {}, {}
            bc_pe = {}
            xsrc3 = xsrc.rearrange("(c p) t -> p c t", p=128)
            ntile = len(tok0s)

            def emit_stats(it):
                bi = it % 2
                t0 = tok0s[it]
                p.add("gpsimd",
                      lambda e, bi=bi, t0=t0: e.dma_start(
                          out=xbf[bi].rearrange("p (c t) -> p c t", c=NCH),
                          in_=xsrc3[:, :, t0:t0 + T]),
                      waits=[(ve, norm_ve.get(it - 2, 0))], inc=ldw, k=16)
                xtok = ldw.n
                p.add("scalar",
                      lambda e, bi=bi: e.activation(xn[bi][:], xbf[bi][:], AF.Square),
                      waits=[(ldw, xtok), (pe, end_tok.get(it - 2, 0))], inc=se)
                sqtok = se.n

                def statsf(e, bi=bi):
                    for c in range(NCH):
                        e.matmul(psA[0:1, :], ones_bf[:],
                                 xbf[bi][:, c * T:(c + 1) * T],
                                 start=(c == 0), stop=(c == NCH - 1))
                    last = None
                    for c in range(NCH):
                        last = e.matmul(psB[0:1, :], ones_bf[:],
                                        xn[bi][:, c * T:(c + 1) * T],
                                        start=(c == 0), stop=(c == NCH - 1))
                    return last
                p.add("tensor", statsf,
                      waits=[(se, sqtok), (ve, norm_ve.get(it - 1, 0))], inc=pe)
                stats_pe[it] = pe.n
                rowtok, _ = ln_rows(p, srow, psA, psB, D, stats_pe[it])
                return rowtok

            def emit_bcast_norm(it, rowtok):
                bi = it % 2
                t0 = tok0s[it]

                def bcast(e):
                    e.matmul(pbc[:, :], onesr_f[:], srow[0:1, 0:T],
                             start=True, stop=True)
                    return e.matmul(psA[:, :], onesr_f[:], srow[0:1, T:2 * T],
                                    start=True, stop=True)
                p.add("tensor", bcast,
                      waits=[(se, rowtok), (ve, norm_ve.get(it - 1, 0))], inc=pe)
                bctok = pe.n

                def normf(e, bi=bi):
                    e.tensor_copy(mbc[:], pbc[:, :])
                    e.tensor_copy(rbc[:], psA[:, :])
                    last = None
                    for c in range(NCH):
                        e.tensor_sub(tmp[:], xbf[bi][:, c * T:(c + 1) * T], mbc[:])
                        last = e.tensor_mul(xn[bi][:, c * T:(c + 1) * T],
                                            tmp[:], rbc[:])
                    return last
                p.add("vector", normf,
                      waits=[(pe, bctok), (pe, end_tok.get(it - 2, 0))], inc=ve)
                norm_ve[it] = ve.n
                if spill_xn:
                    for c in range(NCH):
                        p.add("sync",
                              lambda e, c=c, bi=bi, t0=t0: e.dma_start(
                                  out=xn1_sp[c, :, t0:t0 + T],
                                  in_=xn[bi][:, c * T:(c + 1) * T]),
                              waits=([(ve, norm_ve[it])] if c == 0 else ()),
                              inc=stq, k=16)

            def emit_head(it, h):
                bi = it % 2
                t0 = tok0s[it]
                u = it * H + h
                hb = u % 2

                def headmm(e, bi=bi, hb=hb, h=h):
                    last = None
                    for c in range(NCH):
                        last = e.matmul(
                            psq[hb][:, :],
                            w_sb[:, c * D + h * DH:c * D + (h + 1) * DH],
                            xn[bi][:, c * T:(c + 1) * T],
                            start=(c == 0), stop=(c == NCH - 1))
                    return last
                p.add("tensor", headmm,
                      waits=[(ve, norm_ve[it]), (ldw, wtok),
                             (se, qf_se.get(u - 2, 0))], inc=pe)
                htok = pe.n
                p.add("scalar",
                      lambda e, hb=hb: e.copy(qf[hb][:], psq[hb][:]),
                      waits=[(pe, htok), (ve, rope_ve.get(u - 2, 0))], inc=se)
                qf_se[u] = se.n
                toff = t0 if tbl_cols == S else it * T

                p.add("gpsimd",
                      lambda e, hb=hb, toff=toff: e.tensor_mul(
                          qm[hb][:], qf[hb][:], cosb[:, toff:toff + T]),
                      waits=[(se, qf_se[u]), (ldx, tbltok),
                             (ve, rope_ve.get(u - 2, 0))], inc=po)
                potok = po.n

                def ropef(e, hb=hb, toff=toff):
                    e.stream_shuffle(qs[hb][:], qf[hb][:], SWAP)
                    e.tensor_mul(qs[hb][:], qs[hb][:], sinb[:, toff:toff + T])
                    return e.tensor_add(qo[hb][:], qm[hb][:], qs[hb][:])
                p.add("vector", ropef,
                      waits=[(se, qf_se[u]), (ldx, tbltok), (po, potok),
                             (stq, rope_st.get(u - 2, 0))], inc=ve)
                rope_ve[u] = ve.n
                p.add("sync",
                      lambda e, h=h, t0=t0, hb=hb: e.dma_start(
                          out=o_sp[h, :, t0:t0 + T], in_=qo[hb][:]),
                      waits=[(ve, rope_ve[u])], inc=stq, k=16)
                rope_st[u] = stq.n

            rt = emit_stats(0)
            emit_bcast_norm(0, rt)
            for it in range(ntile):
                rt_next = emit_stats(it + 1) if it + 1 < ntile else None
                for h in range(4):
                    emit_head(it, h)
                if rt_next is not None:
                    emit_bcast_norm(it + 1, rt_next)
                for h in range(4, H):
                    emit_head(it, h)
                end_tok[it] = pe.n
            barrier(p)
            p.play(blk)
            ph.close()

        if "p1a" in phases:
            qk_phase("A", wq, xo, [s * T for s in range(NSLOT)],
                     coso, sino, q_sp, False, TOWN)
        if "p1b" in phases:
            qk_phase("K", wk, xf, [i * T for i in range(8)],
                     cosf, sinf, k_sp, True, S)

        if "p1c" in phases:
            ph = contextlib.ExitStack()
            wv_sb = ph.enter_context(nc.sbuf_tensor("Vw", [128, NCH * D], BF16))
            xnb = [ph.enter_context(nc.sbuf_tensor(f"Vx{i}", [128, NCH * T], BF16)) for i in range(2)]
            vst = [ph.enter_context(nc.sbuf_tensor(f"Vs{i}", [128, D], BF16)) for i in range(2)]
            pv = [ph.enter_context(nc.psum_tensor(f"Vp{i}", [128, D], F32)) for i in range(2)]
            p = Prog()
            p.add("gpsimd", lambda e: e.dma_start(out=wv_sb[:], in_=wv),
                  inc=ldw, k=16)
            wtok = ldw.n
            mmtok, evtok, sptok, xtok = {}, {}, {}, {}
            for it in range(8):
                t0 = it * T
                bi = it % 2
                for c in range(NCH):
                    p.add("sync",
                          lambda e, bi=bi, t0=t0, c=c: e.dma_start(
                              out=xnb[bi][:, c * T:(c + 1) * T],
                              in_=xn1_sp[c, :, t0:t0 + T]),
                          waits=([(pe, mmtok.get((it - 2, 3), 0))] if c == 0 else ()),
                          inc=ldx, k=16)
                xtok[it] = ldx.n
                for sub in range(4):
                    kc = it * 4 + sub
                    rb = kc % 2

                    def vmm(e, bi=bi, sub=sub, rb=rb):
                        last = None
                        for c in range(NCH):
                            lh = xnb[bi][:, c * T + sub * 128:c * T + (sub + 1) * 128]
                            for oc in range(4):
                                last = e.matmul(
                                    pv[rb][:, oc * T:(oc + 1) * T], lh,
                                    wv_sb[:, c * D + oc * T:c * D + (oc + 1) * T],
                                    start=(c == 0), stop=(c == NCH - 1))
                        return last
                    p.add("tensor", vmm,
                          waits=[(ldx, xtok[it]), (ldw, wtok),
                                 (se, evtok.get(kc - 2, 0))], inc=pe)
                    mmtok[(it, sub)] = pe.n
                    p.add("scalar",
                          lambda e, rb=rb: e.copy(vst[rb][:], pv[rb][:]),
                          waits=[(pe, mmtok[(it, sub)]),
                                 (stq, sptok.get(kc - 2, 0))], inc=se)
                    evtok[kc] = se.n
                    for h in range(H):
                        p.add("sync",
                              lambda e, h=h, kc=kc, rb=rb: e.dma_start(
                                  out=v_sp[h, :, kc * 128:(kc + 1) * 128],
                                  in_=vst[rb][:, h * 128:(h + 1) * 128]),
                              waits=([(se, evtok[kc])] if h == 0 else ()),
                              inc=stq, k=16)
                    sptok[kc] = stq.n
            barrier(p)
            p.play(blk)
            ph.close()

        if "p2" in phases or "p3" in phases:
            ph23 = contextlib.ExitStack()
            attn_all = ph23.enter_context(
                nc.sbuf_tensor("att", [128, H * TOWN], BF16))

        if "p2" in phases:
            ph = contextlib.ExitStack()
            khb = [ph.enter_context(nc.sbuf_tensor(f"kh{i}", [128, S], BF16)) for i in range(2)]
            vhb = [ph.enter_context(nc.sbuf_tensor(f"vh{i}", [128, S], BF16)) for i in range(2)]
            qhb = [ph.enter_context(nc.sbuf_tensor(f"qh{i}", [128, TOWN], BF16)) for i in range(2)]
            ptb = [ph.enter_context(nc.sbuf_tensor(f"pt{i}", [128, 1536], BF16)) for i in range(2)]
            msk_sb = ph.enter_context(nc.sbuf_tensor("msk", [128, 8 * T], BF16))
            rs_sb = ph.enter_context(nc.sbuf_tensor("rs", [1, T], F32))
            rs_tmp = ph.enter_context(nc.sbuf_tensor("rst", [1, T], F32))
            rsb_sb = ph.enter_context(nc.sbuf_tensor("rsb", [128, T], F32))
            psc = [ph.enter_context(nc.psum_tensor(f"sc{i}", [128, 1536], F32)) for i in range(2)]
            pav = ph.enter_context(nc.psum_tensor("pav", [128, T], F32))
            pss = ph.enter_context(nc.psum_tensor("pss", [128, T], F32))

            p = Prog()
            p.add("gpsimd", lambda e: e.dma_start(out=msk_sb[:], in_=msk8),
                  inc=ldw, k=16)
            msktok = ldw.n
            hload, hend, exp_se, av_pe, slot_ve = {}, {}, {}, {}, {}

            pend_av = None
            pend_epi = None
            gb = 0

            def flush_av():
                nonlocal pend_av, gb_last
                if pend_av is not None:
                    fn, w = pend_av
                    p.add("tensor", fn, waits=w, inc=pe)
                    av_pe[gb_last] = pe.n
                    pend_av = None

            def flush_epi():
                nonlocal pend_epi
                if pend_epi is None:
                    return
                (eh, es, erstok) = pend_epi
                p.add("tensor",
                      lambda e: e.matmul(pss[:, :], onesr_f[:], rs_sb[:],
                                         start=True, stop=True),
                      waits=[(se, erstok)], inc=pe)
                bctok = pe.n

                def attnf(e, eh=eh, es=es):
                    e.tensor_copy(rsb_sb[:], pss[:, :])
                    return e.tensor_mul(
                        attn_all[:, eh * TOWN + es * T:eh * TOWN + (es + 1) * T],
                        pav[:, :], rsb_sb[:])
                p.add("vector", attnf, waits=[(pe, bctok)], inc=ve)
                slot_ve[(eh, es)] = ve.n
                pend_epi = None

            gb_last = -1
            for h in range(H):
                bi = h % 2
                p.add("sync",
                      lambda e, bi=bi, h=h: e.dma_start(out=khb[bi][:], in_=k_sp[h]),
                      waits=[(pe, hend.get(h - 2, 0))], inc=ldx, k=16)
                p.add("sync",
                      lambda e, bi=bi, h=h: e.dma_start(out=vhb[bi][:], in_=v_sp[h]),
                      inc=ldx, k=16)
                p.add("sync",
                      lambda e, bi=bi, h=h: e.dma_start(out=qhb[bi][:], in_=q_sp[h]),
                      inc=ldx, k=16)
                hload[h] = ldx.n
                for s in range(NSLOT):
                    L = 8 * s + 8
                    kc0 = 0
                    first_of_slot = True
                    while kc0 < L:
                        n = min(3, L - kc0)
                        sb = gb % 2

                        def scmm(e, bi=bi, sb=sb, s=s, kc0=kc0, n=n):
                            last = None
                            for j in range(n):
                                kc = kc0 + j
                                last = e.matmul(
                                    psc[sb][:, j * T:(j + 1) * T],
                                    khb[bi][:, kc * 128:(kc + 1) * 128],
                                    qhb[bi][:, s * T:(s + 1) * T],
                                    start=True, stop=True)
                            return last
                        p.add("tensor", scmm,
                              waits=[(ldx, hload[h]),
                                     (se, exp_se.get(gb - 2, 0))], inc=pe)
                        sctok = pe.n
                        flush_av()
                        flush_epi()
                        p.add("scalar",
                              lambda e, sb=sb, n=n: e.activation(
                                  ptb[sb][:, 0:n * T], psc[sb][:, 0:n * T],
                                  AF.Exp, scale=ISQ),
                              waits=[(pe, sctok), (pe, av_pe.get(gb - 2, 0))],
                              inc=se)
                        exp_se[gb] = se.n
                        need = [j for j in range(n) if kc0 + j >= L - 8]
                        pt_w = (se, exp_se[gb])
                        if need:
                            j0, j1 = need[0], need[-1] + 1
                            mi0 = kc0 + j0 - (L - 8)

                            def maskf(e, sb=sb, j0=j0, j1=j1, mi0=mi0):
                                return e.tensor_mul(
                                    ptb[sb][:, j0 * T:j1 * T],
                                    ptb[sb][:, j0 * T:j1 * T],
                                    msk_sb[:, mi0 * T:(mi0 + j1 - j0) * T])
                            p.add("vector", maskf,
                                  waits=[(se, exp_se[gb]), (ldw, msktok)], inc=ve)
                            pt_w = (ve, ve.n)

                        def avmm(e, bi=bi, sb=sb, n=n, kc0=kc0, L=L):
                            for j in range(n):
                                kc = kc0 + j
                                e.matmul(pav[:, :],
                                         vhb[bi][:, kc * 128:(kc + 1) * 128],
                                         ptb[sb][:, j * T:(j + 1) * T],
                                         start=(kc == 0), stop=(kc == L - 1))
                            last = None
                            for j in range(n):
                                kc = kc0 + j
                                last = e.matmul(pss[0:1, :], ones_bf[:],
                                                ptb[sb][:, j * T:(j + 1) * T],
                                                start=(kc == 0), stop=(kc == L - 1))
                            return last
                        w = [pt_w]
                        if first_of_slot:
                            prev = (h, s - 1) if s else (h - 1, NSLOT - 1)
                            w.append((ve, slot_ve.get(prev, 0)))
                        pend_av = (avmm, w)
                        gb_last = gb
                        first_of_slot = False
                        kc0 += n
                        gb += 1
                    flush_av()

                    def rsf(e):
                        e.activation(rs_tmp[:], pss[0:1, :], AF.Ln)
                        return e.activation(rs_sb[:], rs_tmp[:], AF.Exp,
                                            scale=-1.0)
                    p.add("scalar", rsf, waits=[(pe, av_pe[gb - 1])], inc=se)
                    pend_epi = (h, s, se.n)
                hend[h] = pe.n
            flush_epi()
            if debug_spill:
                p.add("sync",
                      lambda e: e.dma_start(out=attn_dbg, in_=attn_all[:]),
                      waits=[(ve, ve.n)], inc=stq, k=16)
            barrier(p)
            p.play(blk)
            ph.close()

        if "p3" in phases:
            ph = contextlib.ExitStack()
            wo_sb = ph.enter_context(nc.sbuf_tensor("Ow", [128, NCH * D], BF16))
            yf = ph.enter_context(nc.sbuf_tensor("Oyf", [128, NCH * T], F32))
            ybf = [ph.enter_context(nc.sbuf_tensor(f"Oyb{i}", [128, T], BF16)) for i in range(2)]
            y2b = [ph.enter_context(nc.sbuf_tensor(f"Oy2{i}", [128, T], BF16)) for i in range(2)]
            xoc = [ph.enter_context(nc.sbuf_tensor(f"Oxo{i}", [128, T], F32)) for i in range(2)]
            x1bf = [ph.enter_context(nc.sbuf_tensor(f"Oxb{i}", [128, T], BF16)) for i in range(2)]
            x1sq = [ph.enter_context(nc.sbuf_tensor(f"Oxq{i}", [128, T], BF16)) for i in range(2)]
            xn2b = [ph.enter_context(nc.sbuf_tensor(f"On2{i}", [128, T], BF16)) for i in range(2)]
            srow = ph.enter_context(nc.sbuf_tensor("Osr", [1, 4 * T], F32))
            mbc = ph.enter_context(nc.sbuf_tensor("Omb", [128, T], F32))
            rbc = ph.enter_context(nc.sbuf_tensor("Orb", [128, T], F32))
            tmp = ph.enter_context(nc.sbuf_tensor("Otp", [128, T], F32))
            py = [ph.enter_context(nc.psum_tensor(f"Op{i}", [128, T], F32)) for i in range(2)]
            psA = ph.enter_context(nc.psum_tensor("OpA", [128, T], F32))
            psB = ph.enter_context(nc.psum_tensor("OpB", [128, T], F32))
            pbc = ph.enter_context(nc.psum_tensor("OpC", [128, T], F32))

            p = Prog()
            p.add("gpsimd", lambda e: e.dma_start(out=wo_sb[:], in_=wo),
                  inc=ldw, k=16)
            wtok = ldw.n
            prev_cptok3 = 0
            prev_rowtok3 = 0
            for sidx in range(NSLOT):
                t0 = sidx * T
                ev_ve, sq_se, st_pe = {}, {}, {}

                def emit_stf(oc):
                    def stf(e, oc=oc, rb=oc % 2):
                        e.matmul(psA[0:1, :], ones_bf[:], ybf[rb][:],
                                 start=(oc == 0), stop=(oc == NCH - 1))
                        return e.matmul(psB[0:1, :], ones_bf[:], y2b[rb][:],
                                        start=(oc == 0), stop=(oc == NCH - 1))
                    stw = [(se, sq_se[oc])]
                    if oc == 0:
                        stw += [(ve, prev_cptok3), (se, prev_rowtok3)]
                    p.add("tensor", stf, waits=stw, inc=pe)
                    st_pe[oc] = pe.n

                for oc in range(NCH):
                    rb = oc % 2

                    def womm(e, oc=oc, rb=rb, t0=t0):
                        last = None
                        for ic in range(NCH):
                            last = e.matmul(
                                py[rb][:, :],
                                wo_sb[:, ic * D + oc * 128:ic * D + (oc + 1) * 128],
                                attn_all[:, ic * TOWN + t0:ic * TOWN + t0 + T],
                                start=(ic == 0), stop=(ic == NCH - 1))
                        return last
                    p.add("tensor", womm,
                          waits=[(ldw, wtok), (ve, ev_ve.get(oc - 2, 0))], inc=pe)
                    wotok = pe.n
                    if oc >= 1:
                        emit_stf(oc - 1)
                    p.add("vector",
                          lambda e, oc=oc, rb=rb: e.tensor_copy(
                              yf[:, oc * T:(oc + 1) * T], py[rb][:, :]),
                          waits=[(pe, wotok)], inc=ve)
                    ev_ve[oc] = ve.n

                    def sqf(e, oc=oc, rb=rb):
                        e.copy(ybf[rb][:], yf[:, oc * T:(oc + 1) * T])
                        return e.activation(y2b[rb][:], yf[:, oc * T:(oc + 1) * T],
                                            AF.Square)
                    p.add("scalar", sqf,
                          waits=[(ve, ev_ve[oc]), (pe, st_pe.get(oc - 2, 0))],
                          inc=se)
                    sq_se[oc] = se.n
                emit_stf(NCH - 1)
                rowtok, _ = ln_rows(p, srow, psA, psB, D, st_pe[NCH - 1])

                def bcast(e):
                    e.matmul(pbc[:, :], onesr_f[:], srow[0:1, 0:T],
                             start=True, stop=True)
                    return e.matmul(psA[:, :], onesr_f[:], srow[0:1, T:2 * T],
                                    start=True, stop=True)
                p.add("tensor", bcast,
                      waits=[(se, rowtok), (ve, prev_cptok3)], inc=pe)
                bctok = pe.n

                def cpbc(e):
                    e.tensor_copy(mbc[:], pbc[:, :])
                    return e.tensor_copy(rbc[:], psA[:, :])
                p.add("vector", cpbc, waits=[(pe, bctok)], inc=ve)
                cptok = ve.n

                x1_ve, x1_st, sq3_se, st3_pe = {}, {}, {}, {}
                for oc in range(NCH):
                    rb = oc % 2
                    p.add("sync",
                          lambda e, oc=oc, rb=rb, t0=t0: e.dma_start(
                              out=xoc[rb][:],
                              in_=xo[oc * 128:(oc + 1) * 128, t0:t0 + T]),
                          waits=[(ve, x1_ve.get(oc - 2, 0))], inc=ldx, k=16)
                    xotok = ldx.n

                    def x1f(e, oc=oc, rb=rb):
                        e.tensor_sub(tmp[:], yf[:, oc * T:(oc + 1) * T], mbc[:])
                        e.tensor_mul(tmp[:], tmp[:], rbc[:])
                        return e.tensor_add(yf[:, oc * T:(oc + 1) * T],
                                            tmp[:], xoc[rb][:])
                    p.add("vector", x1f,
                          waits=[(ldx, xotok), (ve, cptok), (se, sq_se[oc]),
                                 (stq, x1_st.get(oc - 2, 0))], inc=ve)
                    x1_ve[oc] = ve.n
                    p.add("sync",
                          lambda e, oc=oc, t0=t0: e.dma_start(
                              out=x1_sp[oc, :, t0:t0 + T],
                              in_=yf[:, oc * T:(oc + 1) * T]),
                          waits=[(ve, x1_ve[oc])], inc=stq, k=16)
                    x1_st[oc] = stq.n

                    def sq3(e, oc=oc, rb=rb):
                        e.copy(x1bf[rb][:], yf[:, oc * T:(oc + 1) * T])
                        return e.activation(x1sq[rb][:], yf[:, oc * T:(oc + 1) * T],
                                            AF.Square)
                    p.add("scalar", sq3,
                          waits=[(ve, x1_ve[oc]), (pe, st3_pe.get(oc - 2, 0))],
                          inc=se)
                    sq3_se[oc] = se.n

                    def st3(e, oc=oc, rb=rb):
                        e.matmul(psB[0:1, :], ones_bf[:], x1sq[rb][:],
                                 start=(oc == 0), stop=(oc == NCH - 1))
                        return e.matmul(pbc[0:1, :], ones_bf[:], x1bf[rb][:],
                                        start=(oc == 0), stop=(oc == NCH - 1))
                    p.add("tensor", st3, waits=[(se, sq3_se[oc])], inc=pe)
                    st3_pe[oc] = pe.n
                rowtok3, _ = ln_rows(p, srow, pbc, psB, D, st3_pe[NCH - 1])
                prev_rowtok3 = rowtok3

                def bcast3(e):
                    e.matmul(pbc[:, :], onesr_f[:], srow[0:1, 0:T],
                             start=True, stop=True)
                    return e.matmul(psA[:, :], onesr_f[:], srow[0:1, T:2 * T],
                                    start=True, stop=True)
                p.add("tensor", bcast3, waits=[(se, rowtok3)], inc=pe)
                bctok3 = pe.n
                p.add("vector", cpbc, waits=[(pe, bctok3)], inc=ve)
                cptok3 = ve.n
                prev_cptok3 = cptok3
                n2_ve, n2_st = {}, {}
                for oc in range(NCH):
                    rb = oc % 2

                    def n2f(e, oc=oc, rb=rb):
                        e.tensor_sub(tmp[:], yf[:, oc * T:(oc + 1) * T], mbc[:])
                        return e.tensor_mul(xn2b[rb][:], tmp[:], rbc[:])
                    p.add("vector", n2f,
                          waits=[(ve, cptok3), (stq, n2_st.get(oc - 2, 0))],
                          inc=ve)
                    n2_ve[oc] = ve.n
                    p.add("sync",
                          lambda e, oc=oc, rb=rb, t0=t0: e.dma_start(
                              out=xn2_sp[oc, :, t0:t0 + T], in_=xn2b[rb][:]),
                          waits=[(ve, n2_ve[oc])], inc=stq, k=16)
                    n2_st[oc] = stq.n
            barrier(p)
            p.play(blk)
            ph.close()

        if "p2" in phases or "p3" in phases:
            ph23.close()

        if "p4" in phases:
            ph = contextlib.ExitStack()
            u32 = ph.enter_context(nc.sbuf_tensor("Fu", [128, NCH * T], F32))
            yfb = u32
            xn2t = u32[:].bitcast(BF16)[:, 0:NCH * T]
            hc = ph.enter_context(nc.sbuf_tensor("Fhc", [128, NFC * T], BF16))
            w1s = [ph.enter_context(nc.sbuf_tensor(f"Fw1{i}", [128, W1XC], BF16)) for i in range(4)]
            w2s = [ph.enter_context(nc.sbuf_tensor("Fw2a", [128, W2HA], BF16)),
                   ph.enter_context(nc.sbuf_tensor("Fw2b", [128, W2HB], BF16))]
            a1s = ph.enter_context(nc.sbuf_tensor("Fa1", [128, NCH * ER], BF16))
            wrs = ph.enter_context(nc.sbuf_tensor("Fwr", [128, NCH * E], BF16))
            brc_sb = ph.enter_context(nc.sbuf_tensor("Fbr", [8, 1], F32))
            b1_sb = ph.enter_context(nc.sbuf_tensor("Fb1", [128, NFC], F32))
            b2r_sb = ph.enter_context(nc.sbuf_tensor("Fb2", [1, D], F32))
            l32 = ph.enter_context(nc.sbuf_tensor("Fl32", [32, T], F32))
            t32 = ph.enter_context(nc.sbuf_tensor("Ft32", [32, T], F32))
            m132 = ph.enter_context(nc.sbuf_tensor("Fm1", [32, T], F32))
            l232 = ph.enter_context(nc.sbuf_tensor("Fl2", [32, T], F32))
            m232 = ph.enter_context(nc.sbuf_tensor("Fm2", [32, T], F32))
            oh1 = ph.enter_context(nc.sbuf_tensor("Fo1", [8, T], F32))
            oh2 = ph.enter_context(nc.sbuf_tensor("Fo2", [8, T], F32))
            rows = ph.enter_context(nc.sbuf_tensor("Frw", [1, 4 * T], F32))
            w1bb = ph.enter_context(nc.sbuf_tensor("Fb1b", [128, T], BF16))
            w2bb = ph.enter_context(nc.sbuf_tensor("Fb2b", [128, T], BF16))
            w1b64 = ph.enter_context(nc.sbuf_tensor("F164", [64, T], F32))
            w2b64 = ph.enter_context(nc.sbuf_tensor("F264", [64, T], F32))
            ohx1 = ph.enter_context(nc.sbuf_tensor("Fox1", [64, T], BF16))
            ohx2 = ph.enter_context(nc.sbuf_tensor("Fox2", [64, T], BF16))
            z1b = ph.enter_context(nc.sbuf_tensor("Fz1", [64, T], BF16))
            z12b = ph.enter_context(nc.sbuf_tensor("Fz12", [64, T], BF16))
            zc2b = ph.enter_context(nc.sbuf_tensor("Fzc2", [64, T], BF16))
            tv1 = ph.enter_context(nc.sbuf_tensor("Ftv1", [64, T], F32))
            tv2 = ph.enter_context(nc.sbuf_tensor("Ftv2", [64, T], F32))
            h1b = [ph.enter_context(nc.sbuf_tensor(f"Fh1{i}", [128, T], BF16)) for i in range(2)]
            h2b = [ph.enter_context(nc.sbuf_tensor(f"Fh2{i}", [128, T], BF16)) for i in range(2)]
            htmp = ph.enter_context(nc.sbuf_tensor("Fht", [128, T], BF16))
            ybf = [ph.enter_context(nc.sbuf_tensor(f"Fyb{i}", [128, T], BF16)) for i in range(2)]
            y2b = [ph.enter_context(nc.sbuf_tensor(f"Fy2{i}", [128, T], BF16)) for i in range(2)]
            x1c = [ph.enter_context(nc.sbuf_tensor(f"Fx1{i}", [128, T], F32)) for i in range(2)]
            outb = [ph.enter_context(nc.sbuf_tensor(f"Fot{i}", [128, T], F32)) for i in range(2)]
            srow = ph.enter_context(nc.sbuf_tensor("Fsr", [1, 4 * T], F32))
            mbc = ph.enter_context(nc.sbuf_tensor("Fmb", [128, T], F32))
            rbc = ph.enter_context(nc.sbuf_tensor("Frb", [128, T], F32))
            PA = [ph.enter_context(nc.psum_tensor(f"FA{i}", [128, 1024], F32)) for i in range(2)]
            PB = [ph.enter_context(nc.psum_tensor(f"FB{i}", [64, 1024], F32)) for i in range(2)]

            p = Prog()
            p.add("gpsimd", lambda e: e.dma_start(out=a1s[:], in_=a1r), inc=ldw, k=16)
            p.add("gpsimd", lambda e: e.dma_start(out=wrs[:], in_=wrr), inc=ldw, k=16)
            cwtok = ldw.n
            p.add("sync", lambda e: e.dma_start(out=brc_sb[:], in_=brc), inc=ldx, k=16)
            p.add("sync", lambda e: e.dma_start(out=b1_sb[:], in_=b1cd), inc=ldx, k=16)
            p.add("sync", lambda e: e.dma_start(out=b2r_sb[:], in_=b2rowd), inc=ldx, k=16)
            cstok = ldx.n

            w1cst, w2cst = {}, {}
            prev_cptok4 = 0
            prev_rowtok4 = 0
            prev_o_ve = {}
            prev_base_end = 0
            prev_u22_end = 0
            prev_ymm_end = 0
            out_st = {}

            for ti in range(NSLOT):
                t0 = ti * T
                base_pe, gel1_se, gel2_se, hc_ve, u22_pe = {}, {}, {}, {}, {}
                yev_ve, yst_pe, ymmA_pe, ymmB_pe = {}, {}, {}, {}
                w1tok, w2tok = {}, {}
                for c in range(NCH):
                    p.add("sync",
                          lambda e, c=c, t0=t0: e.dma_start(
                              out=xn2t[:, c * T:(c + 1) * T],
                              in_=xn2_sp[c, :, t0:t0 + T]),
                          waits=([(pe, prev_base_end)] if c == 0 else [])
                                + [(ve, prev_o_ve.get(c, 0))],
                          inc=ldx, k=16)
                xn2tok = ldx.n

                def rlog(e):
                    last = None
                    for c in range(NCH):
                        last = e.matmul(PB[0][0:8, 0:T],
                                        wrs[:, c * E:(c + 1) * E],
                                        xn2t[:, c * T:(c + 1) * T],
                                        start=(c == 0), stop=(c == NCH - 1))
                    return last
                p.add("tensor", rlog,
                      waits=[(ldx, xn2tok), (ldw, cwtok), (se, prev_rowtok4)],
                      inc=pe)
                rltok = pe.n

                def msets(e):
                    e.memset(l32[:], NEG)
                    return e.memset(l232[:], NEG)
                p.add("vector", msets, waits=[(se, prev_rowtok4)], inc=ve)
                mstok = ve.n
                p.add("vector",
                      lambda e: e.tensor_scalar(l32[0:8, :], PB[0][0:8, 0:T],
                                                brc_sb[0:8, 0:1], None, ALU.add),
                      waits=[(pe, rltok), (ve, mstok), (ldx, cstok)], inc=ve)
                lcp = ve.n

                def rtr(e):
                    def tree(src, dst):
                        e.stream_shuffle(t32[:], src[:], [i ^ 4 for i in range(32)])
                        e.tensor_max(dst[:], src[:], t32[:])
                        e.stream_shuffle(t32[:], dst[:], [i ^ 2 for i in range(32)])
                        e.tensor_max(dst[:], dst[:], t32[:])
                        e.stream_shuffle(t32[:], dst[:], [i ^ 1 for i in range(32)])
                        return e.tensor_max(dst[:], dst[:], t32[:])
                    tree(l32, m132)
                    e.tensor_tensor(oh1[:], l32[0:8, :], m132[0:8, :], ALU.is_equal)
                    e.scalar_tensor_tensor(l232[0:8, :], oh1[:], NEG, l32[0:8, :],
                                           ALU.mult, ALU.add)
                    tree(l232, m232)
                    return e.tensor_tensor(oh2[:], l232[0:8, :], m232[0:8, :],
                                           ALU.is_equal)
                p.add("vector", rtr, waits=[], inc=ve)
                rtok = ve.n

                e8b = t32
                def rexp(e):
                    e.activation(e8b[0:8, :], l32[0:8, :], AF.Exp)
                    e.activation(rows[0:1, T:2 * T], m132[0:1, :], AF.Exp)
                    return e.activation(rows[0:1, 2 * T:3 * T], m232[0:1, :], AF.Exp)
                p.add("scalar", rexp, waits=[(ve, rtok)], inc=se)
                extok = se.n
                p.add("tensor",
                      lambda e: e.matmul(PB[1][0:1, 0:T], ones8_f[:], e8b[0:8, :],
                                         start=True, stop=True),
                      waits=[(se, extok), (se, prev_rowtok4), (ve, lcp)], inc=pe)
                sstok = pe.n

                def rwts(e):
                    e.reciprocal(rows[0:1, 0:T], PB[1][0:1, 0:T])
                    e.tensor_mul(rows[0:1, T:2 * T], rows[0:1, T:2 * T],
                                 rows[0:1, 0:T])
                    e.tensor_mul(rows[0:1, 2 * T:3 * T], rows[0:1, 2 * T:3 * T],
                                 rows[0:1, 0:T])
                    return e.tensor_add(rows[0:1, 3 * T:4 * T], rows[0:1, T:2 * T],
                                        rows[0:1, 2 * T:3 * T])
                p.add("vector", rwts, waits=[(pe, sstok)], inc=ve)
                rwtok = ve.n

                def gbc(e):
                    e.matmul(PA[1][:, 0:T], onesr_f[:], rows[0:1, T:2 * T],
                             start=True, stop=True)
                    return e.matmul(PA[0][:, 0:T], onesr_f[:],
                                    rows[0:1, 2 * T:3 * T], start=True, stop=True)
                p.add("tensor", gbc,
                      waits=[(ve, rwtok), (ve, lcp), (ve, prev_cptok4)], inc=pe)
                gbtok = pe.n

                def gcp(e):
                    e.tensor_copy(w1bb[:], PA[1][:, 0:T])
                    e.tensor_copy(w1b64[:], PA[1][0:64, 0:T])
                    e.tensor_copy(w2bb[:], PA[0][:, 0:T])
                    return e.tensor_copy(w2b64[:], PA[0][0:64, 0:T])
                p.add("vector", gcp, waits=[(pe, gbtok)], inc=ve)
                gcptok = ve.n

                def u1f(e):
                    last = None
                    for c in range(NCH):
                        last = e.matmul(PB[1][0:64, 0:T],
                                        a1s[:, c * ER:(c + 1) * ER],
                                        xn2t[:, c * T:(c + 1) * T],
                                        start=(c == 0), stop=(c == NCH - 1))
                    return last
                p.add("tensor", u1f,
                      waits=[(ldx, xn2tok), (ldw, cwtok), (ve, rwtok)],
                      inc=pe)
                u1tok = pe.n

                def ohxf(e):
                    e.matmul(PA[0][0:64, 0:T], x864[:], oh1[:], start=True, stop=True)
                    return e.matmul(PA[1][0:64, 0:T], x864[:], oh2[:],
                                    start=True, stop=True)
                p.add("tensor", ohxf, waits=[(ve, gcptok)], inc=pe)
                ohxtok = pe.n

                def zf(e):
                    e.tensor_copy(ohx1[:], PA[0][0:64, 0:T])
                    e.tensor_copy(ohx2[:], PA[1][0:64, 0:T])
                    e.scalar_tensor_tensor(z1b[:], PB[1][0:64, 0:T], 2.0, ohx1[:],
                                           ALU.mult, ALU.mult)
                    e.scalar_tensor_tensor(tv1[:], PB[1][0:64, 0:T], 2.0, ohx2[:],
                                           ALU.mult, ALU.mult)
                    return e.tensor_sub(z12b[:], tv1[:], z1b[:])
                p.add("vector", zf, waits=[(pe, ohxtok), (pe, u1tok)], inc=ve)
                ztok = ve.n

                u21_pe, z12_pe = {}, {}

                def emit_w1_load(fi):
                    rb = fi % 4
                    wgate = [(pe, u22_pe.get(fi - 4, prev_u22_end if ti else 0))]
                    if ti == 0:
                        p.add("gpsimd",
                              lambda e, fi=fi, rb=rb: e.dma_start(
                                  out=w1s[rb][:], in_=w1x[fi]),
                              waits=wgate + [(stq, w1cst.get(fi - 4, 0))],
                              inc=ldw, k=16)
                        w1tok[fi] = (ldw, ldw.n)
                        p.add("sync",
                              lambda e, fi=fi, rb=rb: e.dma_start(
                                  out=w1cache[fi], in_=w1s[rb][:]),
                              waits=[(ldw, w1tok[fi][1])], inc=stq, k=16)
                        w1cst[fi] = stq.n
                    else:
                        p.add("sync",
                              lambda e, fi=fi, rb=rb: e.dma_start(
                                  out=w1s[rb][:], in_=w1cache[fi]),
                              waits=wgate + [(stq, w1cst[fi])], inc=ldx, k=16)
                        w1tok[fi] = (ldx, ldx.n)

                def emit_base_gel1(fi):
                    rb = fi % 4
                    pa = fi % 2

                    def basef(e, fi=fi, rb=rb, pa=pa):
                        for c in range(NCH):
                            e.matmul(PA[pa][:, 0:T],
                                     w1s[rb][:, c * 128:(c + 1) * 128],
                                     xn2t[:, c * T:(c + 1) * T],
                                     start=(c == 0), stop=False)
                        return e.matmul(PA[pa][:, 0:T], w1s[rb][0:64, D:D + 128],
                                        z1b[:], start=False, stop=False)
                    p.add("tensor", basef,
                          waits=[w1tok[fi], (ve, ztok),
                                 (se, gel2_se.get(fi - 2, 0))], inc=pe)
                    base_pe[fi] = pe.n
                    p.add("scalar",
                          lambda e, fi=fi, pa=pa: e.activation(
                              h1b[fi % 2][:], PA[pa][:, 0:T], AF.Gelu_apprx_tanh,
                              bias=b1_sb[:, fi:fi + 1]),
                          waits=[(pe, base_pe[fi]), (ldx, cstok),
                                 (pe, u21_pe.get(fi - 2, 0)),
                                 (ve, hc_ve.get(fi - 2, 0))], inc=se)
                    gel1_se[fi] = se.n

                def emit_mid(fi):
                    rb = fi % 4
                    pa = fi % 2
                    p.add("tensor",
                          lambda e, fi=fi, rb=rb: e.matmul(
                              PB[0][0:64, 0:T], w1s[rb][:, D + 128:D + 192],
                              h1b[fi % 2][:], start=(fi == 0), stop=(fi == NFC - 1)),
                          waits=[(se, gel1_se[fi])], inc=pe)
                    u21_pe[fi] = pe.n
                    p.add("tensor",
                          lambda e, fi=fi, rb=rb, pa=pa: e.matmul(
                              PA[pa][:, 0:T], w1s[rb][0:64, D:D + 128], z12b[:],
                              start=False, stop=True),
                          waits=[(se, gel1_se[fi])], inc=pe)
                    z12_pe[fi] = pe.n
                    p.add("scalar",
                          lambda e, fi=fi, pa=pa: e.activation(
                              h2b[fi % 2][:], PA[pa][:, 0:T], AF.Gelu_apprx_tanh,
                              bias=b1_sb[:, fi:fi + 1]),
                          waits=[(pe, z12_pe[fi]),
                                 (pe, u22_pe.get(fi - 2, 0)),
                                 (ve, hc_ve.get(fi - 2, 0))], inc=se)
                    gel2_se[fi] = se.n

                    def hcf(e, fi=fi):
                        e.tensor_mul(hc[:, fi * T:(fi + 1) * T],
                                     h1b[fi % 2][:], w1bb[:])
                        e.tensor_mul(htmp[:], h2b[fi % 2][:], w2bb[:])
                        return e.tensor_add(hc[:, fi * T:(fi + 1) * T],
                                            hc[:, fi * T:(fi + 1) * T], htmp[:])
                    p.add("vector", hcf,
                          waits=[(se, gel2_se[fi]), (ve, gcptok)], inc=ve)
                    hc_ve[fi] = ve.n

                def emit_u22(fi):
                    rb = fi % 4
                    p.add("tensor",
                          lambda e, fi=fi, rb=rb: e.matmul(
                              PB[1][0:64, 0:T], w1s[rb][:, D + 128:D + 192],
                              h2b[fi % 2][:], start=(fi == 0), stop=(fi == NFC - 1)),
                          waits=[(se, gel2_se[fi])], inc=pe)
                    u22_pe[fi] = pe.n

                emit_w1_load(0)
                emit_w1_load(1)
                for fi in range(NFC):
                    emit_base_gel1(fi)
                    if fi >= 1:
                        emit_mid(fi - 1)
                    if fi >= 2:
                        emit_u22(fi - 2)
                    if fi + 2 < NFC:
                        emit_w1_load(fi + 2)
                emit_mid(NFC - 1)
                emit_u22(NFC - 2)
                emit_u22(NFC - 1)
                u22tok = u22_pe[NFC - 1]
                prev_base_end = base_pe[NFC - 1]
                prev_u22_end = u22_pe[NFC - 1]

                def zc2f(e):
                    e.tensor_mul(tv1[:], PB[0][0:64, 0:T], ohx1[:])
                    e.scalar_tensor_tensor(tv1[:], tv1[:], 2.0, w1b64[:],
                                           ALU.mult, ALU.mult)
                    e.tensor_mul(tv2[:], PB[1][0:64, 0:T], ohx2[:])
                    e.scalar_tensor_tensor(tv2[:], tv2[:], 2.0, w2b64[:],
                                           ALU.mult, ALU.mult)
                    return e.tensor_add(zc2b[:], tv1[:], tv2[:])
                p.add("vector", zc2f, waits=[(pe, u22tok)], inc=ve)
                zc2tok = ve.n

                ysq_se = {}

                def emit_yst(dc):
                    def yst(e, dc=dc, rb=dc % 2):
                        e.matmul(PB[0][0:1, 0:T], ones_bf[:], ybf[rb][:],
                                 start=(dc == 0), stop=(dc == NCH - 1))
                        return e.matmul(PB[1][0:1, 0:T], ones_bf[:], y2b[rb][:],
                                        start=(dc == 0), stop=(dc == NCH - 1))
                    ystw = [(se, ysq_se[dc])]
                    if dc == 0:
                        ystw.append((ve, zc2tok))
                    p.add("tensor", yst, waits=ystw, inc=pe)
                    yst_pe[dc] = pe.n

                halfspans = ((0, W2HA), (W2HA, W2XC))
                for dc in range(NCH):
                    for half in range(2):
                        lo, hi = halfspans[half]
                        hg = [(pe, (ymmA_pe if half == 0 else ymmB_pe).get(
                            dc - 1, prev_ymm_end))]
                        if ti == 0:
                            p.add("gpsimd",
                                  lambda e, dc=dc, lo=lo, hi=hi, half=half:
                                  e.dma_start(out=w2s[half][:],
                                              in_=w2x[dc][:, lo:hi]),
                                  waits=hg + [(stq, w2cst.get((dc - 1, half), 0))],
                                  inc=ldw, k=16)
                            w2tok[(dc, half)] = (ldw, ldw.n)
                            p.add("sync",
                                  lambda e, dc=dc, lo=lo, hi=hi, half=half:
                                  e.dma_start(out=w2cache[dc][:, lo:hi],
                                              in_=w2s[half][:]),
                                  waits=[(ldw, w2tok[(dc, half)][1])],
                                  inc=stq, k=16)
                            w2cst[(dc, half)] = stq.n
                        else:
                            p.add("scalar",
                                  lambda e, dc=dc, lo=lo, hi=hi, half=half:
                                  e.dma_start(out=w2s[half][:],
                                              in_=w2cache[dc][:, lo:hi]),
                                  waits=hg + [(stq, w2cst[(dc, half)])],
                                  inc=lds, k=16)
                            w2tok[(dc, half)] = (lds, lds.n)

                    def ymm_a(e, dc=dc):
                        pd = PA[dc % 2]
                        last = None
                        for fi in range(32):
                            last = e.matmul(pd[:, 0:T],
                                            w2s[0][:, fi * 128:(fi + 1) * 128],
                                            hc[:, fi * T:(fi + 1) * T],
                                            start=(fi == 0), stop=False)
                        return last
                    p.add("tensor", ymm_a,
                          waits=[w2tok[(dc, 0)], (ve, hc_ve[31]),
                                 (ve, yev_ve.get(dc - 2, 0))], inc=pe)
                    ymmA_pe[dc] = pe.n

                    def ymm_b(e, dc=dc):
                        pd = PA[dc % 2]
                        for fi in range(32, NFC):
                            e.matmul(pd[:, 0:T],
                                     w2s[1][:, (fi - 32) * 128:(fi - 31) * 128],
                                     hc[:, fi * T:(fi + 1) * T],
                                     start=False, stop=False)
                        e.matmul(pd[:, 0:T], w2s[1][0:64, 4096:4224],
                                 zc2b[:], start=False, stop=False)
                        return e.matmul(pd[:, 0:T],
                                        b2r_sb[0:1, dc * 128:(dc + 1) * 128],
                                        rows[0:1, 3 * T:4 * T],
                                        start=False, stop=True)
                    p.add("tensor", ymm_b,
                          waits=[w2tok[(dc, 1)], (ve, zc2tok),
                                 (ve, hc_ve[NFC - 1]), (ldx, cstok)],
                          inc=pe)
                    ymmB_pe[dc] = pe.n
                    ytok = pe.n
                    p.add("vector",
                          lambda e, dc=dc: e.tensor_copy(
                              yfb[:, dc * T:(dc + 1) * T], PA[dc % 2][:, 0:T]),
                          waits=[(pe, ytok)], inc=ve)
                    yev_ve[dc] = ve.n
                    rb = dc % 2

                    def ysq(e, dc=dc, rb=rb):
                        e.copy(ybf[rb][:], yfb[:, dc * T:(dc + 1) * T])
                        return e.activation(y2b[rb][:], yfb[:, dc * T:(dc + 1) * T],
                                            AF.Square)
                    p.add("scalar", ysq,
                          waits=[(ve, yev_ve[dc]), (pe, yst_pe.get(dc - 2, 0))],
                          inc=se)
                    ysq_se[dc] = se.n
                    if dc >= 1:
                        emit_yst(dc - 1)
                emit_yst(NCH - 1)
                rowtok4, _ = ln_rows(p, srow, PB[0], PB[1], D, yst_pe[NCH - 1])
                prev_rowtok4 = rowtok4

                def bcast4(e):
                    e.matmul(PA[0][:, 0:T], onesr_f[:], srow[0:1, 0:T],
                             start=True, stop=True)
                    return e.matmul(PA[1][:, 0:T], onesr_f[:], srow[0:1, T:2 * T],
                                    start=True, stop=True)
                p.add("tensor", bcast4,
                      waits=[(se, rowtok4), (ve, yev_ve[NCH - 1])], inc=pe)
                bctok4 = pe.n

                def cpbc4(e):
                    e.tensor_copy(mbc[:], PA[0][:, 0:T])
                    return e.tensor_copy(rbc[:], PA[1][:, 0:T])
                p.add("vector", cpbc4, waits=[(pe, bctok4)], inc=ve)
                cptok4 = ve.n
                prev_cptok4 = cptok4
                o_ve = {}
                for dc in range(NCH):
                    rb = dc % 2
                    p.add("sync",
                          lambda e, dc=dc, rb=rb, t0=t0: e.dma_start(
                              out=x1c[rb][:], in_=x1_sp[dc, :, t0:t0 + T]),
                          waits=[(ve, o_ve.get(dc - 2, 0))], inc=ldx, k=16)
                    x1tok = ldx.n

                    def outf(e, dc=dc, rb=rb):
                        e.tensor_sub(outb[rb][:], yfb[:, dc * T:(dc + 1) * T], mbc[:])
                        e.tensor_mul(outb[rb][:], outb[rb][:], rbc[:])
                        return e.tensor_add(outb[rb][:], outb[rb][:], x1c[rb][:])
                    p.add("vector", outf,
                          waits=[(ldx, x1tok), (ve, cptok4),
                                 (stq, out_st.get((ti, dc - 2), 0))], inc=ve)
                    o_ve[dc] = ve.n
                    p.add("sync",
                          lambda e, dc=dc, rb=rb, t0=t0: e.dma_start(
                              out=out[dc * 128:(dc + 1) * 128, t0:t0 + T],
                              in_=outb[rb][:]),
                          waits=[(ve, o_ve[dc])], inc=stq, k=16)
                    out_st[(ti, dc)] = stq.n
                prev_o_ve = dict(o_ve)
                prev_ymm_end = ymmB_pe[NCH - 1]
            barrier(p)
            p.play(blk)
            ph.close()

    return nc


def _prep_inputs(ins):
    x = np.asarray(ins["L0_attn_x"], np.float32)
    Wq = np.asarray(ins["L0_attn_Wq"], np.float32)
    Wk = np.asarray(ins["L0_attn_Wk"], np.float32)
    Wv = np.asarray(ins["L0_attn_Wv"], np.float32)
    Wo = np.asarray(ins["L0_attn_Wo"], np.float32)
    W_r = np.asarray(ins["L0_ffn_W_r"], np.float32)
    b_r = np.asarray(ins["L0_ffn_b_r"], np.float32)
    W1 = np.asarray(ins["L0_ffn_W1"], np.float32)
    b1 = np.asarray(ins["L0_ffn_b1"], np.float32)
    W2 = np.asarray(ins["L0_ffn_W2"], np.float32)
    b2 = np.asarray(ins["L0_ffn_b2"], np.float32)
    A1 = np.asarray(ins["L0_ffn_A1"], np.float32)
    B1 = np.asarray(ins["L0_ffn_B1"], np.float32)
    A2 = np.asarray(ins["L0_ffn_A2"], np.float32)
    B2 = np.asarray(ins["L0_ffn_B2"], np.float32)

    def chunk_major(WT):
        din, o = WT.shape
        return np.ascontiguousarray(
            WT.reshape(din // 128, 128, o).transpose(1, 0, 2).reshape(128, -1))

    wqh = chunk_major(np.ascontiguousarray(Wq.T))
    wkh = chunk_major(np.ascontiguousarray(Wk.T))
    wvh = chunk_major(np.ascontiguousarray(Wv.T))
    woh = chunk_major(np.ascontiguousarray(Wo.T))
    wrh = chunk_major(np.ascontiguousarray(W_r.T))
    a1h = chunk_major(np.ascontiguousarray(A1.reshape(ER, D).T))

    W1T = np.ascontiguousarray(W1.T)
    w1base = W1T.reshape(NCH, 128, NFC, 128).transpose(2, 1, 0, 3).reshape(NFC, 128, D)
    b1n = B1.transpose(0, 2, 1).reshape(ER, FF)
    a2t = np.ascontiguousarray(A2.reshape(ER, FF).T)
    w1xh = np.zeros((NFC, 128, W1XC), np.float32)
    w1xh[:, :, :D] = w1base
    w1xh[:, :ER, D:D + 128] = b1n.reshape(ER, NFC, 128).transpose(1, 0, 2)
    w1xh[:, :, D + 128:D + 192] = a2t.reshape(NFC, 128, ER)

    W2T = np.ascontiguousarray(W2.T)
    w2base = W2T.reshape(NFC, 128, NCH, 128).transpose(2, 1, 0, 3).reshape(NCH, 128, FF)
    b2n = B2.transpose(0, 2, 1).reshape(ER, D)
    w2xh = np.zeros((NCH, 128, W2XC), np.float32)
    w2xh[:, :, :FF] = w2base
    w2xh[:, :ER, FF:FF + 128] = b2n.reshape(ER, NCH, 128).transpose(1, 0, 2)

    b1ch = np.ascontiguousarray(b1.reshape(NFC, 128).T)
    b2row = np.ascontiguousarray(b2.reshape(1, D))
    brcol = np.ascontiguousarray(b_r.reshape(E, 1))

    j = np.arange(0, DH, 2, dtype=np.float64) / DH
    inv = 10000.0 ** (-j)
    tpos = np.arange(S, dtype=np.float64)
    ang = inv[:, None] * tpos[None, :]
    cosa = np.cos(ang)
    sina = np.sin(ang)
    cosfh = np.empty((128, S), np.float32)
    sinfh = np.empty((128, S), np.float32)
    cosfh[0::2] = cosa
    cosfh[1::2] = cosa
    sinfh[0::2] = -sina
    sinfh[1::2] = sina

    kk = np.arange(128)[:, None]
    qq = np.arange(T)[None, :]

    def masks(par):
        m = np.zeros((128, 8, T), np.float32)
        for mi in range(8):
            if par == 0:
                if mi < 4:
                    m[:, mi, :] = (128 * mi + kk <= qq)
            else:
                if mi < 4:
                    m[:, mi, :] = 1.0
                else:
                    m[:, mi, :] = (128 * (mi - 4) + kk <= qq)
        return np.ascontiguousarray(m.reshape(128, 8 * T))

    x864h = np.zeros((E, ER), np.float32)
    for e_ in range(E):
        x864h[e_, e_ * R:(e_ + 1) * R] = 1.0

    in_maps, own_cols_all = [], []
    for c in range(8):
        b, par = c // 2, c % 2
        xfm = np.ascontiguousarray(x[b].T)
        own = np.concatenate(
            [np.arange(T) + T * (2 * s + par) for s in range(NSLOT)])
        own_cols_all.append(own)
        in_maps.append(dict(
            xf=xfm, xo=np.ascontiguousarray(xfm[:, own]),
            wq=wqh, wk=wkh, wv=wvh, wo=woh,
            w1x=w1xh, w2x=w2xh, a1r=a1h, wrr=wrh, brc=brcol, b1cd=b1ch,
            b2rowd=b2row, cosf=cosfh, sinf=sinfh,
            coso=np.ascontiguousarray(cosfh[:, own]),
            sino=np.ascontiguousarray(sinfh[:, own]),
            msk8=masks(par), x864d=x864h,
        ))
    return in_maps, own_cols_all


def kernel(**inputs):
    in_maps, own_cols = _prep_inputs(inputs)
    nc = build_program()
    res = run_bass_kernel_spmd(nc, in_maps, core_ids=list(range(8)))
    outp = np.empty((B, S, D), np.float32)
    for c in range(8):
        b = c // 2
        y = np.asarray(res.results[c]["out"])
        outp[b, own_cols[c], :] = y.T
    return outp


# version 22 (final)
# speedup vs baseline: 1.0894x; 1.0894x over previous
import contextlib
import math

import numpy as np
import ml_dtypes

import concourse.bass as bass
import concourse.mybir as mybir
from concourse.bass_utils import run_bass_kernel_spmd

F32 = mybir.dt.float32
BF16 = mybir.dt.bfloat16
AF = mybir.ActivationFunctionType
ALU = mybir.AluOpType

B, S, D, H, DH = 4, 4096, 2048, 16, 128
FF, E, R, ER = 8192, 8, 8, 64
NCH = 16
NFC = 64
NSLOT = 4
TOWN = 2048
T = 512
EPS = 1e-5
ISQ = 1.0 / math.sqrt(DH)
NEG = -1.0e30
W1XC = D + 192
W2XC = FF + 128
W2HA = 4096
W2HB = W2XC - W2HA
ENGINES = ("sync", "gpsimd", "tensor", "vector", "scalar")


class Sem:
    def __init__(self, ctx, nc, name):
        self.h = ctx.enter_context(nc.semaphore(name))
        self.n = 0


class Prog:

    def __init__(self):
        self.ops = {e: [] for e in ENGINES}

    def add(self, engine, fn=None, waits=(), inc=None, k=1):
        w = [(s, v) for (s, v) in waits if v > 0]
        self.ops[engine].append((w, fn, inc, k))
        if inc is not None:
            inc.n += k
            return inc.n
        return None

    def play(self, blk):
        for name in ENGINES:
            ops = self.ops[name]
            if not ops:
                continue

            def run(eng, ops=ops):
                for waits, fn, inc, k in ops:
                    for (sem, val) in waits:
                        eng.wait_ge(sem.h, val)
                    instr = fn(eng) if fn is not None else None
                    if inc is not None:
                        if instr is None:
                            instr = eng.nop()
                        instr.then_inc(inc.h, k)
            getattr(blk, name)(run)


def build_program(phases=("p1a", "p1b", "p1c", "p2", "p3", "p4"),
                  debug_spill=False):
    nc = bass.Bass("TRN2", target_bir_lowering=False)
    dt_spill = "ExternalOutput" if debug_spill else "Internal"

    xf = nc.dram_tensor("xf", [D, S], F32, kind="ExternalInput").ap()
    xo = nc.dram_tensor("xo", [D, TOWN], F32, kind="ExternalInput").ap()
    wq = nc.dram_tensor("wq", [128, NCH * D], F32, kind="ExternalInput").ap()
    wk = nc.dram_tensor("wk", [128, NCH * D], F32, kind="ExternalInput").ap()
    wv = nc.dram_tensor("wv", [128, NCH * D], F32, kind="ExternalInput").ap()
    wo = nc.dram_tensor("wo", [128, NCH * D], F32, kind="ExternalInput").ap()
    w1x = nc.dram_tensor("w1x", [NFC, 128, W1XC], F32, kind="ExternalInput").ap()
    w2x = nc.dram_tensor("w2x", [NCH, 128, W2XC], F32, kind="ExternalInput").ap()
    a1r = nc.dram_tensor("a1r", [128, NCH * ER], F32, kind="ExternalInput").ap()
    wrr = nc.dram_tensor("wrr", [128, NCH * E], F32, kind="ExternalInput").ap()
    brc = nc.dram_tensor("brc", [E, 1], F32, kind="ExternalInput").ap()
    b1cd = nc.dram_tensor("b1cd", [128, NFC], F32, kind="ExternalInput").ap()
    b2rowd = nc.dram_tensor("b2rowd", [1, D], F32, kind="ExternalInput").ap()
    cosf = nc.dram_tensor("cosf", [128, S], F32, kind="ExternalInput").ap()
    sinf = nc.dram_tensor("sinf", [128, S], F32, kind="ExternalInput").ap()
    coso = nc.dram_tensor("coso", [128, TOWN], F32, kind="ExternalInput").ap()
    sino = nc.dram_tensor("sino", [128, TOWN], F32, kind="ExternalInput").ap()
    msk8 = nc.dram_tensor("msk8", [128, 8 * T], F32, kind="ExternalInput").ap()
    x864d = nc.dram_tensor("x864d", [8, ER], F32, kind="ExternalInput").ap()

    q_sp = nc.dram_tensor("q_sp", [H, 128, TOWN], BF16, kind=dt_spill).ap()
    k_sp = nc.dram_tensor("k_sp", [H, 128, S], BF16, kind=dt_spill).ap()
    v_sp = nc.dram_tensor("v_sp", [H, 128, S], BF16, kind=dt_spill).ap()
    xn1_sp = nc.dram_tensor("xn1_sp", [NCH, 128, S], BF16, kind=dt_spill).ap()
    x1_sp = nc.dram_tensor("x1_sp", [NCH, 128, TOWN], F32, kind=dt_spill).ap()
    xn2_sp = nc.dram_tensor("xn2_sp", [NCH, 128, TOWN], BF16, kind=dt_spill).ap()
    w1cache = nc.dram_tensor("w1cache", [NFC, 128, W1XC], BF16, kind="Internal").ap()
    w2cache = nc.dram_tensor("w2cache", [NCH, 128, W2XC], BF16, kind="Internal").ap()
    attn_dbg = None
    srow_dbg = None
    if debug_spill:
        attn_dbg = nc.dram_tensor("attn_dbg", [128, H * TOWN], BF16,
                                  kind="ExternalOutput").ap()
        srow_dbg = nc.dram_tensor("srow_dbg", [1, 4 * T], F32,
                                  kind="ExternalOutput").ap()
    out = nc.dram_tensor("out", [D, TOWN], F32, kind="ExternalOutput").ap()

    with contextlib.ExitStack() as ctx:
        ec = ctx.enter_context
        ones_bf = ec(nc.sbuf_tensor("ones_bf", [128, 1], BF16))
        onesr_f = ec(nc.sbuf_tensor("onesr_f", [1, 128], F32))
        ones8_f = ec(nc.sbuf_tensor("ones8_f", [8, 1], F32))
        x864 = ec(nc.sbuf_tensor("x864", [8, ER], F32))


        bar = Sem(ctx, nc, "bar")
        ldx = Sem(ctx, nc, "ldx")
        ldw = Sem(ctx, nc, "ldw")
        stq = Sem(ctx, nc, "stq")
        pe = Sem(ctx, nc, "pe")
        ve = Sem(ctx, nc, "ve")
        se = Sem(ctx, nc, "se")
        po = Sem(ctx, nc, "po")

        blk = ec(nc.Block())

        def barrier(p):
            p.add("sync", lambda e: e.nop(),
                  waits=[(ldx, ldx.n), (stq, stq.n)], inc=bar)
            p.add("gpsimd", lambda e: e.nop(), waits=[(ldw, ldw.n)], inc=bar)
            p.add("tensor", lambda e: e.nop(), inc=bar)
            p.add("vector", lambda e: e.nop(), inc=bar)
            p.add("scalar", lambda e: e.nop(), inc=bar)
            t = bar.n
            for en in ENGINES:
                p.add(en, None, waits=[(bar, t)])

        def ln_rows(p, srow, psA, psB, n, stats_tok):
            def rows1(e):
                e.mul(srow[0:1, 0:T], psA[0:1, 0:T], 1.0 / n)
                e.activation(srow[0:1, 2 * T:3 * T], srow[0:1, 0:T], AF.Square)
                return e.mul(srow[0:1, 3 * T:4 * T], psB[0:1, 0:T], 1.0 / n)
            p.add("scalar", rows1, waits=[(pe, stats_tok)], inc=se)
            pretok = se.n
            p.add("vector",
                  lambda e: e.scalar_tensor_tensor(
                      srow[0:1, 2 * T:3 * T], srow[0:1, 3 * T:4 * T], EPS,
                      srow[0:1, 2 * T:3 * T], ALU.add, ALU.subtract),
                  waits=[(se, pretok)], inc=ve)
            vartok = ve.n

            def rows2(e):
                e.activation(srow[0:1, 3 * T:4 * T], srow[0:1, 2 * T:3 * T],
                             AF.Ln)
                return e.activation(srow[0:1, T:2 * T], srow[0:1, 3 * T:4 * T],
                                    AF.Exp, scale=-0.5)
            p.add("scalar", rows2, waits=[(ve, vartok)], inc=se)
            return se.n, pretok

        p = Prog()

        def _consts(e):
            e.memset(ones_bf[:], 1.0)
            e.memset(onesr_f[:], 1.0)
            return e.memset(ones8_f[:], 1.0)
        p.add("vector", _consts, inc=ve)
        p.add("sync", lambda e: e.dma_start(out=x864[:], in_=x864d), inc=ldx, k=16)
        barrier(p)
        p.play(blk)

        SWAP = [i ^ 1 for i in range(32)]

        def qk_phase(name, w_dram, xsrc, tok0s, cos_d, sin_d, o_sp, spill_xn,
                     tbl_cols):
            ph = contextlib.ExitStack()
            w_sb = ph.enter_context(nc.sbuf_tensor(name + "w", [128, NCH * D], BF16))
            xbf = [ph.enter_context(nc.sbuf_tensor(f"{name}xb{i}", [128, NCH * T], BF16)) for i in range(2)]
            xn = [ph.enter_context(nc.sbuf_tensor(f"{name}xn{i}", [128, NCH * T], BF16)) for i in range(2)]
            srow = ph.enter_context(nc.sbuf_tensor(name + "sr", [1, 4 * T], F32))
            mbc = ph.enter_context(nc.sbuf_tensor(name + "mb", [128, T], F32))
            rbc = ph.enter_context(nc.sbuf_tensor(name + "rb", [128, T], F32))
            cosb = ph.enter_context(nc.sbuf_tensor(name + "co", [128, tbl_cols], F32))
            sinb = ph.enter_context(nc.sbuf_tensor(name + "si", [128, tbl_cols], F32))
            qf = [ph.enter_context(nc.sbuf_tensor(f"{name}qf{i}", [128, T], F32)) for i in range(2)]
            qs = [ph.enter_context(nc.sbuf_tensor(f"{name}qs{i}", [128, T], F32)) for i in range(2)]
            qm = [ph.enter_context(nc.sbuf_tensor(f"{name}qm{i}", [128, T], F32)) for i in range(2)]
            qo = [ph.enter_context(nc.sbuf_tensor(f"{name}qo{i}", [128, T], BF16)) for i in range(2)]
            tmp = ph.enter_context(nc.sbuf_tensor(name + "tp", [128, T], F32))
            psq = [ph.enter_context(nc.psum_tensor(f"{name}pq{i}", [128, T], F32)) for i in range(2)]
            psA = ph.enter_context(nc.psum_tensor(name + "pA", [128, T], F32))
            psB = ph.enter_context(nc.psum_tensor(name + "pB", [128, T], F32))
            pbc = ph.enter_context(nc.psum_tensor(name + "pC", [128, T], F32))

            p = Prog()
            p.add("gpsimd", lambda e: e.dma_start(out=w_sb[:], in_=w_dram),
                  inc=ldw, k=16)
            wtok = ldw.n
            p.add("sync", lambda e: e.dma_start(out=cosb[:], in_=cos_d), inc=ldx, k=16)
            p.add("sync", lambda e: e.dma_start(out=sinb[:], in_=sin_d), inc=ldx, k=16)
            tbltok = ldx.n

            end_tok, rope_ve, rope_st, qf_se, norm_ve, stats_pe = {}, {}, {}, {}, {}, {}
            bc_pe = {}
            xsrc3 = xsrc.rearrange("(c p) t -> p c t", p=128)
            ntile = len(tok0s)

            def emit_stats(it):
                bi = it % 2
                t0 = tok0s[it]
                p.add("gpsimd",
                      lambda e, bi=bi, t0=t0: e.dma_start(
                          out=xbf[bi].rearrange("p (c t) -> p c t", c=NCH),
                          in_=xsrc3[:, :, t0:t0 + T]),
                      waits=[(ve, norm_ve.get(it - 2, 0))], inc=ldw, k=16)
                xtok = ldw.n
                p.add("scalar",
                      lambda e, bi=bi: e.activation(xn[bi][:], xbf[bi][:], AF.Square),
                      waits=[(ldw, xtok), (pe, end_tok.get(it - 2, 0))], inc=se)
                sqtok = se.n

                def statsf(e, bi=bi):
                    for c in range(NCH):
                        e.matmul(psA[0:1, :], ones_bf[:],
                                 xbf[bi][:, c * T:(c + 1) * T],
                                 start=(c == 0), stop=(c == NCH - 1))
                    last = None
                    for c in range(NCH):
                        last = e.matmul(psB[0:1, :], ones_bf[:],
                                        xn[bi][:, c * T:(c + 1) * T],
                                        start=(c == 0), stop=(c == NCH - 1))
                    return last
                p.add("tensor", statsf,
                      waits=[(se, sqtok), (ve, norm_ve.get(it - 1, 0))], inc=pe)
                stats_pe[it] = pe.n
                rowtok, _ = ln_rows(p, srow, psA, psB, D, stats_pe[it])
                return rowtok

            def emit_bcast_norm(it, rowtok):
                bi = it % 2
                t0 = tok0s[it]

                def bcast(e):
                    e.matmul(pbc[:, :], onesr_f[:], srow[0:1, 0:T],
                             start=True, stop=True)
                    return e.matmul(psA[:, :], onesr_f[:], srow[0:1, T:2 * T],
                                    start=True, stop=True)
                p.add("tensor", bcast,
                      waits=[(se, rowtok), (ve, norm_ve.get(it - 1, 0))], inc=pe)
                bctok = pe.n

                def normf(e, bi=bi):
                    e.tensor_copy(mbc[:], pbc[:, :])
                    e.tensor_copy(rbc[:], psA[:, :])
                    last = None
                    for c in range(NCH):
                        e.tensor_sub(tmp[:], xbf[bi][:, c * T:(c + 1) * T], mbc[:])
                        last = e.tensor_mul(xn[bi][:, c * T:(c + 1) * T],
                                            tmp[:], rbc[:])
                    return last
                p.add("vector", normf,
                      waits=[(pe, bctok), (pe, end_tok.get(it - 2, 0))], inc=ve)
                norm_ve[it] = ve.n
                if spill_xn:
                    for c in range(NCH):
                        p.add("sync",
                              lambda e, c=c, bi=bi, t0=t0: e.dma_start(
                                  out=xn1_sp[c, :, t0:t0 + T],
                                  in_=xn[bi][:, c * T:(c + 1) * T]),
                              waits=([(ve, norm_ve[it])] if c == 0 else ()),
                              inc=stq, k=16)

            def emit_head(it, h):
                bi = it % 2
                t0 = tok0s[it]
                u = it * H + h
                hb = u % 2

                def headmm(e, bi=bi, hb=hb, h=h):
                    last = None
                    for c in range(NCH):
                        last = e.matmul(
                            psq[hb][:, :],
                            w_sb[:, c * D + h * DH:c * D + (h + 1) * DH],
                            xn[bi][:, c * T:(c + 1) * T],
                            start=(c == 0), stop=(c == NCH - 1))
                    return last
                p.add("tensor", headmm,
                      waits=[(ve, norm_ve[it]), (ldw, wtok),
                             (se, qf_se.get(u - 2, 0))], inc=pe)
                htok = pe.n
                p.add("scalar",
                      lambda e, hb=hb: e.copy(qf[hb][:], psq[hb][:]),
                      waits=[(pe, htok), (ve, rope_ve.get(u - 2, 0))], inc=se)
                qf_se[u] = se.n
                toff = t0 if tbl_cols == S else it * T

                p.add("gpsimd",
                      lambda e, hb=hb, toff=toff: e.tensor_mul(
                          qm[hb][:], qf[hb][:], cosb[:, toff:toff + T]),
                      waits=[(se, qf_se[u]), (ldx, tbltok),
                             (ve, rope_ve.get(u - 2, 0))], inc=po)
                potok = po.n

                def ropef(e, hb=hb, toff=toff):
                    e.stream_shuffle(qs[hb][:], qf[hb][:], SWAP)
                    e.tensor_mul(qs[hb][:], qs[hb][:], sinb[:, toff:toff + T])
                    return e.tensor_add(qo[hb][:], qm[hb][:], qs[hb][:])
                p.add("vector", ropef,
                      waits=[(se, qf_se[u]), (ldx, tbltok), (po, potok),
                             (stq, rope_st.get(u - 2, 0))], inc=ve)
                rope_ve[u] = ve.n
                p.add("sync",
                      lambda e, h=h, t0=t0, hb=hb: e.dma_start(
                          out=o_sp[h, :, t0:t0 + T], in_=qo[hb][:]),
                      waits=[(ve, rope_ve[u])], inc=stq, k=16)
                rope_st[u] = stq.n

            rt = emit_stats(0)
            emit_bcast_norm(0, rt)
            for it in range(ntile):
                rt_next = emit_stats(it + 1) if it + 1 < ntile else None
                for h in range(4):
                    emit_head(it, h)
                if rt_next is not None:
                    emit_bcast_norm(it + 1, rt_next)
                for h in range(4, H):
                    emit_head(it, h)
                end_tok[it] = pe.n
            barrier(p)
            p.play(blk)
            ph.close()

        if "p1a" in phases:
            qk_phase("A", wq, xo, [s * T for s in range(NSLOT)],
                     coso, sino, q_sp, False, TOWN)
        if "p1b" in phases:
            qk_phase("K", wk, xf, [i * T for i in range(8)],
                     cosf, sinf, k_sp, True, S)

        if "p1c" in phases:
            ph = contextlib.ExitStack()
            wv_sb = ph.enter_context(nc.sbuf_tensor("Vw", [128, NCH * D], BF16))
            xnb = [ph.enter_context(nc.sbuf_tensor(f"Vx{i}", [128, NCH * T], BF16)) for i in range(2)]
            vst = [ph.enter_context(nc.sbuf_tensor(f"Vs{i}", [128, D], BF16)) for i in range(2)]
            pv = [ph.enter_context(nc.psum_tensor(f"Vp{i}", [128, D], F32)) for i in range(2)]
            p = Prog()
            p.add("gpsimd", lambda e: e.dma_start(out=wv_sb[:], in_=wv),
                  inc=ldw, k=16)
            wtok = ldw.n
            mmtok, evtok, sptok, xtok = {}, {}, {}, {}
            for it in range(8):
                t0 = it * T
                bi = it % 2
                for c in range(NCH):
                    p.add("sync",
                          lambda e, bi=bi, t0=t0, c=c: e.dma_start(
                              out=xnb[bi][:, c * T:(c + 1) * T],
                              in_=xn1_sp[c, :, t0:t0 + T]),
                          waits=([(pe, mmtok.get((it - 2, 3), 0))] if c == 0 else ()),
                          inc=ldx, k=16)
                xtok[it] = ldx.n
                for sub in range(4):
                    kc = it * 4 + sub
                    rb = kc % 2

                    def vmm(e, bi=bi, sub=sub, rb=rb):
                        last = None
                        for c in range(NCH):
                            lh = xnb[bi][:, c * T + sub * 128:c * T + (sub + 1) * 128]
                            for oc in range(4):
                                last = e.matmul(
                                    pv[rb][:, oc * T:(oc + 1) * T], lh,
                                    wv_sb[:, c * D + oc * T:c * D + (oc + 1) * T],
                                    start=(c == 0), stop=(c == NCH - 1))
                        return last
                    p.add("tensor", vmm,
                          waits=[(ldx, xtok[it]), (ldw, wtok),
                                 (se, evtok.get(kc - 2, 0))], inc=pe)
                    mmtok[(it, sub)] = pe.n
                    p.add("scalar",
                          lambda e, rb=rb: e.copy(vst[rb][:], pv[rb][:]),
                          waits=[(pe, mmtok[(it, sub)]),
                                 (stq, sptok.get(kc - 2, 0))], inc=se)
                    evtok[kc] = se.n
                    for h in range(H):
                        p.add("sync",
                              lambda e, h=h, kc=kc, rb=rb: e.dma_start(
                                  out=v_sp[h, :, kc * 128:(kc + 1) * 128],
                                  in_=vst[rb][:, h * 128:(h + 1) * 128]),
                              waits=([(se, evtok[kc])] if h == 0 else ()),
                              inc=stq, k=16)
                    sptok[kc] = stq.n
            barrier(p)
            p.play(blk)
            ph.close()

        if "p2" in phases or "p3" in phases:
            ph23 = contextlib.ExitStack()
            attn_all = ph23.enter_context(
                nc.sbuf_tensor("att", [128, H * TOWN], BF16))

        if "p2" in phases:
            ph = contextlib.ExitStack()
            khb = [ph.enter_context(nc.sbuf_tensor(f"kh{i}", [128, S], BF16)) for i in range(3)]
            vhb = [ph.enter_context(nc.sbuf_tensor(f"vh{i}", [128, S], BF16)) for i in range(3)]
            qhb = [ph.enter_context(nc.sbuf_tensor(f"qh{i}", [128, TOWN], BF16)) for i in range(3)]
            ptb = [ph.enter_context(nc.sbuf_tensor(f"pt{i}", [128, 1536], BF16)) for i in range(2)]
            msk_sb = ph.enter_context(nc.sbuf_tensor("msk", [128, 8 * T], BF16))
            rs_sb = ph.enter_context(nc.sbuf_tensor("rs", [1, T], F32))
            rs_tmp = ph.enter_context(nc.sbuf_tensor("rst", [1, T], F32))
            rsb_sb = ph.enter_context(nc.sbuf_tensor("rsb", [128, T], F32))
            psc = [ph.enter_context(nc.psum_tensor(f"sc{i}", [128, 1536], F32)) for i in range(2)]
            pav = ph.enter_context(nc.psum_tensor("pav", [128, T], F32))
            pss = ph.enter_context(nc.psum_tensor("pss", [128, T], F32))

            p = Prog()
            p.add("gpsimd", lambda e: e.dma_start(out=msk_sb[:], in_=msk8),
                  inc=ldw, k=16)
            msktok = ldw.n
            hload, hend, exp_se, av_pe, slot_ve = {}, {}, {}, {}, {}

            pend_av = None
            pend_epi = None
            gb = 0

            def flush_av():
                nonlocal pend_av, gb_last
                if pend_av is not None:
                    fn, w = pend_av
                    p.add("tensor", fn, waits=w, inc=pe)
                    av_pe[gb_last] = pe.n
                    pend_av = None

            def flush_epi():
                nonlocal pend_epi
                if pend_epi is None:
                    return
                (eh, es, erstok) = pend_epi
                p.add("tensor",
                      lambda e: e.matmul(pss[:, :], onesr_f[:], rs_sb[:],
                                         start=True, stop=True),
                      waits=[(se, erstok)], inc=pe)
                bctok = pe.n

                def attnf(e, eh=eh, es=es):
                    e.tensor_copy(rsb_sb[:], pss[:, :])
                    return e.tensor_mul(
                        attn_all[:, eh * TOWN + es * T:eh * TOWN + (es + 1) * T],
                        pav[:, :], rsb_sb[:])
                p.add("vector", attnf, waits=[(pe, bctok)], inc=ve)
                slot_ve[(eh, es)] = ve.n
                pend_epi = None

            gb_last = -1
            for h in range(H):
                bi = h % 3
                p.add("sync",
                      lambda e, bi=bi, h=h: e.dma_start(out=khb[bi][:], in_=k_sp[h]),
                      waits=[(pe, hend.get(h - 3, 0))], inc=ldx, k=16)
                p.add("sync",
                      lambda e, bi=bi, h=h: e.dma_start(out=vhb[bi][:], in_=v_sp[h]),
                      inc=ldx, k=16)
                p.add("sync",
                      lambda e, bi=bi, h=h: e.dma_start(out=qhb[bi][:], in_=q_sp[h]),
                      inc=ldx, k=16)
                hload[h] = ldx.n
                for s in range(NSLOT):
                    L = 8 * s + 8
                    kc0 = 0
                    first_of_slot = True
                    while kc0 < L:
                        n = min(3, L - kc0)
                        sb = gb % 2

                        def scmm(e, bi=bi, sb=sb, s=s, kc0=kc0, n=n):
                            last = None
                            for j in range(n):
                                kc = kc0 + j
                                last = e.matmul(
                                    psc[sb][:, j * T:(j + 1) * T],
                                    khb[bi][:, kc * 128:(kc + 1) * 128],
                                    qhb[bi][:, s * T:(s + 1) * T],
                                    start=True, stop=True)
                            return last
                        p.add("tensor", scmm,
                              waits=[(ldx, hload[h]),
                                     (se, exp_se.get(gb - 2, 0))], inc=pe)
                        sctok = pe.n
                        flush_av()
                        flush_epi()
                        p.add("scalar",
                              lambda e, sb=sb, n=n: e.activation(
                                  ptb[sb][:, 0:n * T], psc[sb][:, 0:n * T],
                                  AF.Exp, scale=ISQ),
                              waits=[(pe, sctok), (pe, av_pe.get(gb - 2, 0))],
                              inc=se)
                        exp_se[gb] = se.n
                        need = [j for j in range(n) if kc0 + j >= L - 8]
                        pt_w = (se, exp_se[gb])
                        if need:
                            j0, j1 = need[0], need[-1] + 1
                            mi0 = kc0 + j0 - (L - 8)

                            def maskf(e, sb=sb, j0=j0, j1=j1, mi0=mi0):
                                return e.tensor_mul(
                                    ptb[sb][:, j0 * T:j1 * T],
                                    ptb[sb][:, j0 * T:j1 * T],
                                    msk_sb[:, mi0 * T:(mi0 + j1 - j0) * T])
                            p.add("vector", maskf,
                                  waits=[(se, exp_se[gb]), (ldw, msktok)], inc=ve)
                            pt_w = (ve, ve.n)

                        def avmm(e, bi=bi, sb=sb, n=n, kc0=kc0, L=L):
                            for j in range(n):
                                kc = kc0 + j
                                e.matmul(pav[:, :],
                                         vhb[bi][:, kc * 128:(kc + 1) * 128],
                                         ptb[sb][:, j * T:(j + 1) * T],
                                         start=(kc == 0), stop=(kc == L - 1))
                            last = None
                            for j in range(n):
                                kc = kc0 + j
                                last = e.matmul(pss[0:1, :], ones_bf[:],
                                                ptb[sb][:, j * T:(j + 1) * T],
                                                start=(kc == 0), stop=(kc == L - 1))
                            return last
                        w = [pt_w]
                        if first_of_slot:
                            prev = (h, s - 1) if s else (h - 1, NSLOT - 1)
                            w.append((ve, slot_ve.get(prev, 0)))
                        pend_av = (avmm, w)
                        gb_last = gb
                        first_of_slot = False
                        kc0 += n
                        gb += 1
                    flush_av()

                    def rsf(e):
                        e.activation(rs_tmp[:], pss[0:1, :], AF.Ln)
                        return e.activation(rs_sb[:], rs_tmp[:], AF.Exp,
                                            scale=-1.0)
                    p.add("scalar", rsf, waits=[(pe, av_pe[gb - 1])], inc=se)
                    pend_epi = (h, s, se.n)
                hend[h] = pe.n
            flush_epi()
            if debug_spill:
                p.add("sync",
                      lambda e: e.dma_start(out=attn_dbg, in_=attn_all[:]),
                      waits=[(ve, ve.n)], inc=stq, k=16)
            barrier(p)
            p.play(blk)
            ph.close()

        if "p3" in phases:
            ph = contextlib.ExitStack()
            wo_sb = ph.enter_context(nc.sbuf_tensor("Ow", [128, NCH * D], BF16))
            yf = ph.enter_context(nc.sbuf_tensor("Oyf", [128, NCH * T], F32))
            ybf = [ph.enter_context(nc.sbuf_tensor(f"Oyb{i}", [128, T], BF16)) for i in range(2)]
            y2b = [ph.enter_context(nc.sbuf_tensor(f"Oy2{i}", [128, T], BF16)) for i in range(2)]
            xoc = [ph.enter_context(nc.sbuf_tensor(f"Oxo{i}", [128, T], F32)) for i in range(2)]
            x1bf = [ph.enter_context(nc.sbuf_tensor(f"Oxb{i}", [128, T], BF16)) for i in range(2)]
            x1sq = [ph.enter_context(nc.sbuf_tensor(f"Oxq{i}", [128, T], BF16)) for i in range(2)]
            xn2b = [ph.enter_context(nc.sbuf_tensor(f"On2{i}", [128, T], BF16)) for i in range(2)]
            srow = ph.enter_context(nc.sbuf_tensor("Osr", [1, 4 * T], F32))
            mbc = ph.enter_context(nc.sbuf_tensor("Omb", [128, T], F32))
            rbc = ph.enter_context(nc.sbuf_tensor("Orb", [128, T], F32))
            tmp = ph.enter_context(nc.sbuf_tensor("Otp", [128, T], F32))
            py = [ph.enter_context(nc.psum_tensor(f"Op{i}", [128, T], F32)) for i in range(2)]
            psA = ph.enter_context(nc.psum_tensor("OpA", [128, T], F32))
            psB = ph.enter_context(nc.psum_tensor("OpB", [128, T], F32))
            pbc = ph.enter_context(nc.psum_tensor("OpC", [128, T], F32))

            p = Prog()
            p.add("gpsimd", lambda e: e.dma_start(out=wo_sb[:], in_=wo),
                  inc=ldw, k=16)
            wtok = ldw.n
            prev_cptok3 = 0
            prev_rowtok3 = 0
            for sidx in range(NSLOT):
                t0 = sidx * T
                ev_ve, sq_se, st_pe = {}, {}, {}

                def emit_stf(oc):
                    def stf(e, oc=oc, rb=oc % 2):
                        e.matmul(psA[0:1, :], ones_bf[:], ybf[rb][:],
                                 start=(oc == 0), stop=(oc == NCH - 1))
                        return e.matmul(psB[0:1, :], ones_bf[:], y2b[rb][:],
                                        start=(oc == 0), stop=(oc == NCH - 1))
                    stw = [(se, sq_se[oc])]
                    if oc == 0:
                        stw += [(ve, prev_cptok3), (se, prev_rowtok3)]
                    p.add("tensor", stf, waits=stw, inc=pe)
                    st_pe[oc] = pe.n

                for oc in range(NCH):
                    rb = oc % 2

                    def womm(e, oc=oc, rb=rb, t0=t0):
                        last = None
                        for ic in range(NCH):
                            last = e.matmul(
                                py[rb][:, :],
                                wo_sb[:, ic * D + oc * 128:ic * D + (oc + 1) * 128],
                                attn_all[:, ic * TOWN + t0:ic * TOWN + t0 + T],
                                start=(ic == 0), stop=(ic == NCH - 1))
                        return last
                    p.add("tensor", womm,
                          waits=[(ldw, wtok), (ve, ev_ve.get(oc - 2, 0))], inc=pe)
                    wotok = pe.n
                    if oc >= 1:
                        emit_stf(oc - 1)
                    p.add("vector",
                          lambda e, oc=oc, rb=rb: e.tensor_copy(
                              yf[:, oc * T:(oc + 1) * T], py[rb][:, :]),
                          waits=[(pe, wotok)], inc=ve)
                    ev_ve[oc] = ve.n

                    def sqf(e, oc=oc, rb=rb):
                        e.copy(ybf[rb][:], yf[:, oc * T:(oc + 1) * T])
                        return e.activation(y2b[rb][:], yf[:, oc * T:(oc + 1) * T],
                                            AF.Square)
                    p.add("scalar", sqf,
                          waits=[(ve, ev_ve[oc]), (pe, st_pe.get(oc - 2, 0))],
                          inc=se)
                    sq_se[oc] = se.n
                emit_stf(NCH - 1)
                rowtok, _ = ln_rows(p, srow, psA, psB, D, st_pe[NCH - 1])

                def bcast(e):
                    e.matmul(pbc[:, :], onesr_f[:], srow[0:1, 0:T],
                             start=True, stop=True)
                    return e.matmul(psA[:, :], onesr_f[:], srow[0:1, T:2 * T],
                                    start=True, stop=True)
                p.add("tensor", bcast,
                      waits=[(se, rowtok), (ve, prev_cptok3)], inc=pe)
                bctok = pe.n

                def cpbc(e):
                    e.tensor_copy(mbc[:], pbc[:, :])
                    return e.tensor_copy(rbc[:], psA[:, :])
                p.add("vector", cpbc, waits=[(pe, bctok)], inc=ve)
                cptok = ve.n

                x1_ve, x1_st, sq3_se, st3_pe = {}, {}, {}, {}
                for oc in range(NCH):
                    rb = oc % 2
                    p.add("sync",
                          lambda e, oc=oc, rb=rb, t0=t0: e.dma_start(
                              out=xoc[rb][:],
                              in_=xo[oc * 128:(oc + 1) * 128, t0:t0 + T]),
                          waits=[(ve, x1_ve.get(oc - 2, 0))], inc=ldx, k=16)
                    xotok = ldx.n

                    def x1f(e, oc=oc, rb=rb):
                        e.tensor_sub(tmp[:], yf[:, oc * T:(oc + 1) * T], mbc[:])
                        e.tensor_mul(tmp[:], tmp[:], rbc[:])
                        return e.tensor_add(yf[:, oc * T:(oc + 1) * T],
                                            tmp[:], xoc[rb][:])
                    p.add("vector", x1f,
                          waits=[(ldx, xotok), (ve, cptok), (se, sq_se[oc]),
                                 (stq, x1_st.get(oc - 2, 0))], inc=ve)
                    x1_ve[oc] = ve.n
                    p.add("sync",
                          lambda e, oc=oc, t0=t0: e.dma_start(
                              out=x1_sp[oc, :, t0:t0 + T],
                              in_=yf[:, oc * T:(oc + 1) * T]),
                          waits=[(ve, x1_ve[oc])], inc=stq, k=16)
                    x1_st[oc] = stq.n

                    def sq3(e, oc=oc, rb=rb):
                        e.copy(x1bf[rb][:], yf[:, oc * T:(oc + 1) * T])
                        return e.activation(x1sq[rb][:], yf[:, oc * T:(oc + 1) * T],
                                            AF.Square)
                    p.add("scalar", sq3,
                          waits=[(ve, x1_ve[oc]), (pe, st3_pe.get(oc - 2, 0))],
                          inc=se)
                    sq3_se[oc] = se.n

                    def st3(e, oc=oc, rb=rb):
                        e.matmul(psB[0:1, :], ones_bf[:], x1sq[rb][:],
                                 start=(oc == 0), stop=(oc == NCH - 1))
                        return e.matmul(pbc[0:1, :], ones_bf[:], x1bf[rb][:],
                                        start=(oc == 0), stop=(oc == NCH - 1))
                    p.add("tensor", st3, waits=[(se, sq3_se[oc])], inc=pe)
                    st3_pe[oc] = pe.n
                rowtok3, _ = ln_rows(p, srow, pbc, psB, D, st3_pe[NCH - 1])
                prev_rowtok3 = rowtok3

                def bcast3(e):
                    e.matmul(pbc[:, :], onesr_f[:], srow[0:1, 0:T],
                             start=True, stop=True)
                    return e.matmul(psA[:, :], onesr_f[:], srow[0:1, T:2 * T],
                                    start=True, stop=True)
                p.add("tensor", bcast3, waits=[(se, rowtok3)], inc=pe)
                bctok3 = pe.n
                p.add("vector", cpbc, waits=[(pe, bctok3)], inc=ve)
                cptok3 = ve.n
                prev_cptok3 = cptok3
                n2_ve, n2_st = {}, {}
                for oc in range(NCH):
                    rb = oc % 2

                    def n2f(e, oc=oc, rb=rb):
                        e.tensor_sub(tmp[:], yf[:, oc * T:(oc + 1) * T], mbc[:])
                        return e.tensor_mul(xn2b[rb][:], tmp[:], rbc[:])
                    p.add("vector", n2f,
                          waits=[(ve, cptok3), (stq, n2_st.get(oc - 2, 0))],
                          inc=ve)
                    n2_ve[oc] = ve.n
                    p.add("sync",
                          lambda e, oc=oc, rb=rb, t0=t0: e.dma_start(
                              out=xn2_sp[oc, :, t0:t0 + T], in_=xn2b[rb][:]),
                          waits=[(ve, n2_ve[oc])], inc=stq, k=16)
                    n2_st[oc] = stq.n
            barrier(p)
            p.play(blk)
            ph.close()

        if "p2" in phases or "p3" in phases:
            ph23.close()

        if "p4" in phases:
            ph = contextlib.ExitStack()
            u32 = ph.enter_context(nc.sbuf_tensor("Fu", [128, NCH * T], F32))
            yfb = u32
            xn2t = u32[:].bitcast(BF16)[:, 0:NCH * T]
            hc = ph.enter_context(nc.sbuf_tensor("Fhc", [128, NFC * T], BF16))
            w1s = [ph.enter_context(nc.sbuf_tensor(f"Fw1{i}", [128, W1XC], BF16)) for i in range(4)]
            w2s = [ph.enter_context(nc.sbuf_tensor("Fw2a", [128, W2HA], BF16)),
                   ph.enter_context(nc.sbuf_tensor("Fw2b", [128, W2HB], BF16))]
            a1s = ph.enter_context(nc.sbuf_tensor("Fa1", [128, NCH * ER], BF16))
            wrs = ph.enter_context(nc.sbuf_tensor("Fwr", [128, NCH * E], BF16))
            brc_sb = ph.enter_context(nc.sbuf_tensor("Fbr", [8, 1], F32))
            b1_sb = ph.enter_context(nc.sbuf_tensor("Fb1", [128, NFC], F32))
            b2r_sb = ph.enter_context(nc.sbuf_tensor("Fb2", [1, D], F32))
            l32 = ph.enter_context(nc.sbuf_tensor("Fl32", [32, T], F32))
            t32 = ph.enter_context(nc.sbuf_tensor("Ft32", [32, T], F32))
            m132 = ph.enter_context(nc.sbuf_tensor("Fm1", [32, T], F32))
            l232 = ph.enter_context(nc.sbuf_tensor("Fl2", [32, T], F32))
            m232 = ph.enter_context(nc.sbuf_tensor("Fm2", [32, T], F32))
            oh1 = ph.enter_context(nc.sbuf_tensor("Fo1", [8, T], F32))
            oh2 = ph.enter_context(nc.sbuf_tensor("Fo2", [8, T], F32))
            rows = ph.enter_context(nc.sbuf_tensor("Frw", [1, 4 * T], F32))
            w1bb = ph.enter_context(nc.sbuf_tensor("Fb1b", [128, T], BF16))
            w2bb = ph.enter_context(nc.sbuf_tensor("Fb2b", [128, T], BF16))
            w1b64 = ph.enter_context(nc.sbuf_tensor("F164", [64, T], F32))
            w2b64 = ph.enter_context(nc.sbuf_tensor("F264", [64, T], F32))
            ohx1 = ph.enter_context(nc.sbuf_tensor("Fox1", [64, T], BF16))
            ohx2 = ph.enter_context(nc.sbuf_tensor("Fox2", [64, T], BF16))
            z1b = ph.enter_context(nc.sbuf_tensor("Fz1", [64, T], BF16))
            z12b = ph.enter_context(nc.sbuf_tensor("Fz12", [64, T], BF16))
            zc2b = ph.enter_context(nc.sbuf_tensor("Fzc2", [64, T], BF16))
            tv1 = ph.enter_context(nc.sbuf_tensor("Ftv1", [64, T], F32))
            tv2 = ph.enter_context(nc.sbuf_tensor("Ftv2", [64, T], F32))
            h1b = [ph.enter_context(nc.sbuf_tensor(f"Fh1{i}", [128, T], BF16)) for i in range(2)]
            h2b = [ph.enter_context(nc.sbuf_tensor(f"Fh2{i}", [128, T], BF16)) for i in range(2)]
            htmp = ph.enter_context(nc.sbuf_tensor("Fht", [128, T], BF16))
            ybf = [ph.enter_context(nc.sbuf_tensor(f"Fyb{i}", [128, T], BF16)) for i in range(2)]
            y2b = [ph.enter_context(nc.sbuf_tensor(f"Fy2{i}", [128, T], BF16)) for i in range(2)]
            x1c = [ph.enter_context(nc.sbuf_tensor(f"Fx1{i}", [128, T], F32)) for i in range(2)]
            outb = [ph.enter_context(nc.sbuf_tensor(f"Fot{i}", [128, T], F32)) for i in range(2)]
            srow = ph.enter_context(nc.sbuf_tensor("Fsr", [1, 4 * T], F32))
            mbc = ph.enter_context(nc.sbuf_tensor("Fmb", [128, T], F32))
            rbc = ph.enter_context(nc.sbuf_tensor("Frb", [128, T], F32))
            PA = [ph.enter_context(nc.psum_tensor(f"FA{i}", [128, 1024], F32)) for i in range(2)]
            PB = [ph.enter_context(nc.psum_tensor(f"FB{i}", [64, 1024], F32)) for i in range(2)]

            p = Prog()
            p.add("gpsimd", lambda e: e.dma_start(out=a1s[:], in_=a1r), inc=ldw, k=16)
            p.add("gpsimd", lambda e: e.dma_start(out=wrs[:], in_=wrr), inc=ldw, k=16)
            cwtok = ldw.n
            p.add("sync", lambda e: e.dma_start(out=brc_sb[:], in_=brc), inc=ldx, k=16)
            p.add("sync", lambda e: e.dma_start(out=b1_sb[:], in_=b1cd), inc=ldx, k=16)
            p.add("sync", lambda e: e.dma_start(out=b2r_sb[:], in_=b2rowd), inc=ldx, k=16)
            cstok = ldx.n

            w1cst, w2cst = {}, {}
            prev_cptok4 = 0
            prev_rowtok4 = 0
            prev_o_ve = {}
            prev_base_end = 0
            prev_u22_end = 0
            prev_ymm_end = 0
            out_st = {}

            for ti in range(NSLOT):
                t0 = ti * T
                base_pe, gel1_se, gel2_se, hc_ve, u22_pe = {}, {}, {}, {}, {}
                yev_ve, yst_pe, ymmA_pe, ymmB_pe = {}, {}, {}, {}
                w1tok, w2tok = {}, {}
                for c in range(NCH):
                    p.add("sync",
                          lambda e, c=c, t0=t0: e.dma_start(
                              out=xn2t[:, c * T:(c + 1) * T],
                              in_=xn2_sp[c, :, t0:t0 + T]),
                          waits=([(pe, prev_base_end)] if c == 0 else [])
                                + [(ve, prev_o_ve.get(c, 0))],
                          inc=ldx, k=16)
                xn2tok = ldx.n

                def rlog(e):
                    last = None
                    for c in range(NCH):
                        last = e.matmul(PB[0][0:8, 0:T],
                                        wrs[:, c * E:(c + 1) * E],
                                        xn2t[:, c * T:(c + 1) * T],
                                        start=(c == 0), stop=(c == NCH - 1))
                    return last
                p.add("tensor", rlog,
                      waits=[(ldx, xn2tok), (ldw, cwtok), (se, prev_rowtok4)],
                      inc=pe)
                rltok = pe.n

                def msets(e):
                    e.memset(l32[:], NEG)
                    return e.memset(l232[:], NEG)
                p.add("vector", msets, waits=[(se, prev_rowtok4)], inc=ve)
                mstok = ve.n
                p.add("vector",
                      lambda e: e.tensor_scalar(l32[0:8, :], PB[0][0:8, 0:T],
                                                brc_sb[0:8, 0:1], None, ALU.add),
                      waits=[(pe, rltok), (ve, mstok), (ldx, cstok)], inc=ve)
                lcp = ve.n

                def rtr(e):
                    def tree(src, dst):
                        e.stream_shuffle(t32[:], src[:], [i ^ 4 for i in range(32)])
                        e.tensor_max(dst[:], src[:], t32[:])
                        e.stream_shuffle(t32[:], dst[:], [i ^ 2 for i in range(32)])
                        e.tensor_max(dst[:], dst[:], t32[:])
                        e.stream_shuffle(t32[:], dst[:], [i ^ 1 for i in range(32)])
                        return e.tensor_max(dst[:], dst[:], t32[:])
                    tree(l32, m132)
                    e.tensor_tensor(oh1[:], l32[0:8, :], m132[0:8, :], ALU.is_equal)
                    e.scalar_tensor_tensor(l232[0:8, :], oh1[:], NEG, l32[0:8, :],
                                           ALU.mult, ALU.add)
                    tree(l232, m232)
                    return e.tensor_tensor(oh2[:], l232[0:8, :], m232[0:8, :],
                                           ALU.is_equal)
                p.add("vector", rtr, waits=[], inc=ve)
                rtok = ve.n

                e8b = t32
                def rexp(e):
                    e.activation(e8b[0:8, :], l32[0:8, :], AF.Exp)
                    e.activation(rows[0:1, T:2 * T], m132[0:1, :], AF.Exp)
                    return e.activation(rows[0:1, 2 * T:3 * T], m232[0:1, :], AF.Exp)
                p.add("scalar", rexp, waits=[(ve, rtok)], inc=se)
                extok = se.n
                p.add("tensor",
                      lambda e: e.matmul(PB[1][0:1, 0:T], ones8_f[:], e8b[0:8, :],
                                         start=True, stop=True),
                      waits=[(se, extok), (se, prev_rowtok4), (ve, lcp)], inc=pe)
                sstok = pe.n

                def rwts(e):
                    e.reciprocal(rows[0:1, 0:T], PB[1][0:1, 0:T])
                    e.tensor_mul(rows[0:1, T:2 * T], rows[0:1, T:2 * T],
                                 rows[0:1, 0:T])
                    e.tensor_mul(rows[0:1, 2 * T:3 * T], rows[0:1, 2 * T:3 * T],
                                 rows[0:1, 0:T])
                    return e.tensor_add(rows[0:1, 3 * T:4 * T], rows[0:1, T:2 * T],
                                        rows[0:1, 2 * T:3 * T])
                p.add("vector", rwts, waits=[(pe, sstok)], inc=ve)
                rwtok = ve.n

                def gbc(e):
                    e.matmul(PA[1][:, 0:T], onesr_f[:], rows[0:1, T:2 * T],
                             start=True, stop=True)
                    return e.matmul(PA[0][:, 0:T], onesr_f[:],
                                    rows[0:1, 2 * T:3 * T], start=True, stop=True)
                p.add("tensor", gbc,
                      waits=[(ve, rwtok), (ve, lcp), (ve, prev_cptok4)], inc=pe)
                gbtok = pe.n

                def gcp(e):
                    e.tensor_copy(w1bb[:], PA[1][:, 0:T])
                    e.tensor_copy(w1b64[:], PA[1][0:64, 0:T])
                    e.tensor_copy(w2bb[:], PA[0][:, 0:T])
                    return e.tensor_copy(w2b64[:], PA[0][0:64, 0:T])
                p.add("vector", gcp, waits=[(pe, gbtok)], inc=ve)
                gcptok = ve.n

                def u1f(e):
                    last = None
                    for c in range(NCH):
                        last = e.matmul(PB[1][0:64, 0:T],
                                        a1s[:, c * ER:(c + 1) * ER],
                                        xn2t[:, c * T:(c + 1) * T],
                                        start=(c == 0), stop=(c == NCH - 1))
                    return last
                p.add("tensor", u1f,
                      waits=[(ldx, xn2tok), (ldw, cwtok), (ve, rwtok)],
                      inc=pe)
                u1tok = pe.n

                def ohxf(e):
                    e.matmul(PA[0][0:64, 0:T], x864[:], oh1[:], start=True, stop=True)
                    return e.matmul(PA[1][0:64, 0:T], x864[:], oh2[:],
                                    start=True, stop=True)
                p.add("tensor", ohxf, waits=[(ve, gcptok)], inc=pe)
                ohxtok = pe.n

                def zf(e):
                    e.tensor_copy(ohx1[:], PA[0][0:64, 0:T])
                    e.tensor_copy(ohx2[:], PA[1][0:64, 0:T])
                    e.scalar_tensor_tensor(z1b[:], PB[1][0:64, 0:T], 2.0, ohx1[:],
                                           ALU.mult, ALU.mult)
                    e.scalar_tensor_tensor(tv1[:], PB[1][0:64, 0:T], 2.0, ohx2[:],
                                           ALU.mult, ALU.mult)
                    return e.tensor_sub(z12b[:], tv1[:], z1b[:])
                p.add("vector", zf, waits=[(pe, ohxtok), (pe, u1tok)], inc=ve)
                ztok = ve.n

                u21_pe, z12_pe = {}, {}

                def emit_w1_load(fi):
                    rb = fi % 4
                    wgate = [(pe, u22_pe.get(fi - 4, prev_u22_end if ti else 0))]
                    if ti == 0:
                        p.add("gpsimd",
                              lambda e, fi=fi, rb=rb: e.dma_start(
                                  out=w1s[rb][:], in_=w1x[fi]),
                              waits=wgate + [(stq, w1cst.get(fi - 4, 0))],
                              inc=ldw, k=16)
                        w1tok[fi] = (ldw, ldw.n)
                        p.add("sync",
                              lambda e, fi=fi, rb=rb: e.dma_start(
                                  out=w1cache[fi], in_=w1s[rb][:]),
                              waits=[(ldw, w1tok[fi][1])], inc=stq, k=16)
                        w1cst[fi] = stq.n
                    else:
                        p.add("sync",
                              lambda e, fi=fi, rb=rb: e.dma_start(
                                  out=w1s[rb][:], in_=w1cache[fi]),
                              waits=wgate + [(stq, w1cst[fi])], inc=ldx, k=16)
                        w1tok[fi] = (ldx, ldx.n)

                def emit_base_gel1(fi):
                    rb = fi % 4
                    pa = fi % 2

                    def basef(e, fi=fi, rb=rb, pa=pa):
                        for c in range(NCH):
                            e.matmul(PA[pa][:, 0:T],
                                     w1s[rb][:, c * 128:(c + 1) * 128],
                                     xn2t[:, c * T:(c + 1) * T],
                                     start=(c == 0), stop=False)
                        return e.matmul(PA[pa][:, 0:T], w1s[rb][0:64, D:D + 128],
                                        z1b[:], start=False, stop=False)
                    p.add("tensor", basef,
                          waits=[w1tok[fi], (ve, ztok),
                                 (se, gel2_se.get(fi - 2, 0))], inc=pe)
                    base_pe[fi] = pe.n
                    p.add("scalar",
                          lambda e, fi=fi, pa=pa: e.activation(
                              h1b[fi % 2][:], PA[pa][:, 0:T], AF.Gelu_apprx_tanh,
                              bias=b1_sb[:, fi:fi + 1]),
                          waits=[(pe, base_pe[fi]), (ldx, cstok),
                                 (pe, u21_pe.get(fi - 2, 0)),
                                 (ve, hc_ve.get(fi - 2, 0))], inc=se)
                    gel1_se[fi] = se.n

                def emit_mid(fi):
                    rb = fi % 4
                    pa = fi % 2
                    p.add("tensor",
                          lambda e, fi=fi, rb=rb: e.matmul(
                              PB[0][0:64, 0:T], w1s[rb][:, D + 128:D + 192],
                              h1b[fi % 2][:], start=(fi == 0), stop=(fi == NFC - 1)),
                          waits=[(se, gel1_se[fi])], inc=pe)
                    u21_pe[fi] = pe.n
                    p.add("tensor",
                          lambda e, fi=fi, rb=rb, pa=pa: e.matmul(
                              PA[pa][:, 0:T], w1s[rb][0:64, D:D + 128], z12b[:],
                              start=False, stop=True),
                          waits=[(se, gel1_se[fi])], inc=pe)
                    z12_pe[fi] = pe.n
                    p.add("scalar",
                          lambda e, fi=fi, pa=pa: e.activation(
                              h2b[fi % 2][:], PA[pa][:, 0:T], AF.Gelu_apprx_tanh,
                              bias=b1_sb[:, fi:fi + 1]),
                          waits=[(pe, z12_pe[fi]),
                                 (pe, u22_pe.get(fi - 2, 0)),
                                 (ve, hc_ve.get(fi - 2, 0))], inc=se)
                    gel2_se[fi] = se.n

                    def hcf(e, fi=fi):
                        e.tensor_mul(hc[:, fi * T:(fi + 1) * T],
                                     h1b[fi % 2][:], w1bb[:])
                        e.tensor_mul(htmp[:], h2b[fi % 2][:], w2bb[:])
                        return e.tensor_add(hc[:, fi * T:(fi + 1) * T],
                                            hc[:, fi * T:(fi + 1) * T], htmp[:])
                    p.add("vector", hcf,
                          waits=[(se, gel2_se[fi]), (ve, gcptok)], inc=ve)
                    hc_ve[fi] = ve.n

                def emit_u22(fi):
                    rb = fi % 4
                    p.add("tensor",
                          lambda e, fi=fi, rb=rb: e.matmul(
                              PB[1][0:64, 0:T], w1s[rb][:, D + 128:D + 192],
                              h2b[fi % 2][:], start=(fi == 0), stop=(fi == NFC - 1)),
                          waits=[(se, gel2_se[fi])], inc=pe)
                    u22_pe[fi] = pe.n

                emit_w1_load(0)
                emit_w1_load(1)
                for fi in range(NFC):
                    emit_base_gel1(fi)
                    if fi >= 1:
                        emit_mid(fi - 1)
                    if fi >= 2:
                        emit_u22(fi - 2)
                    if fi + 2 < NFC:
                        emit_w1_load(fi + 2)
                emit_mid(NFC - 1)
                emit_u22(NFC - 2)
                emit_u22(NFC - 1)
                u22tok = u22_pe[NFC - 1]
                prev_base_end = base_pe[NFC - 1]
                prev_u22_end = u22_pe[NFC - 1]

                def zc2f(e):
                    e.tensor_mul(tv1[:], PB[0][0:64, 0:T], ohx1[:])
                    e.scalar_tensor_tensor(tv1[:], tv1[:], 2.0, w1b64[:],
                                           ALU.mult, ALU.mult)
                    e.tensor_mul(tv2[:], PB[1][0:64, 0:T], ohx2[:])
                    e.scalar_tensor_tensor(tv2[:], tv2[:], 2.0, w2b64[:],
                                           ALU.mult, ALU.mult)
                    return e.tensor_add(zc2b[:], tv1[:], tv2[:])
                p.add("vector", zc2f, waits=[(pe, u22tok)], inc=ve)
                zc2tok = ve.n

                ysq_se = {}

                def emit_yst(dc):
                    def yst(e, dc=dc, rb=dc % 2):
                        e.matmul(PB[0][0:1, 0:T], ones_bf[:], ybf[rb][:],
                                 start=(dc == 0), stop=(dc == NCH - 1))
                        return e.matmul(PB[1][0:1, 0:T], ones_bf[:], y2b[rb][:],
                                        start=(dc == 0), stop=(dc == NCH - 1))
                    ystw = [(se, ysq_se[dc])]
                    if dc == 0:
                        ystw.append((ve, zc2tok))
                    p.add("tensor", yst, waits=ystw, inc=pe)
                    yst_pe[dc] = pe.n

                halfspans = ((0, W2HA), (W2HA, W2XC))
                for dc in range(NCH):
                    for half in range(2):
                        lo, hi = halfspans[half]
                        hg = [(pe, (ymmA_pe if half == 0 else ymmB_pe).get(
                            dc - 1, prev_ymm_end))]
                        if ti == 0:
                            p.add("gpsimd",
                                  lambda e, dc=dc, lo=lo, hi=hi, half=half:
                                  e.dma_start(out=w2s[half][:],
                                              in_=w2x[dc][:, lo:hi]),
                                  waits=hg + [(stq, w2cst.get((dc - 1, half), 0))],
                                  inc=ldw, k=16)
                            w2tok[(dc, half)] = (ldw, ldw.n)
                            p.add("sync",
                                  lambda e, dc=dc, lo=lo, hi=hi, half=half:
                                  e.dma_start(out=w2cache[dc][:, lo:hi],
                                              in_=w2s[half][:]),
                                  waits=[(ldw, w2tok[(dc, half)][1])],
                                  inc=stq, k=16)
                            w2cst[(dc, half)] = stq.n
                        else:
                            p.add("sync",
                                  lambda e, dc=dc, lo=lo, hi=hi, half=half:
                                  e.dma_start(out=w2s[half][:],
                                              in_=w2cache[dc][:, lo:hi]),
                                  waits=hg + [(stq, w2cst[(dc, half)])],
                                  inc=ldx, k=16)
                            w2tok[(dc, half)] = (ldx, ldx.n)

                    def ymm_a(e, dc=dc):
                        pd = PA[dc % 2]
                        last = None
                        for fi in range(32):
                            last = e.matmul(pd[:, 0:T],
                                            w2s[0][:, fi * 128:(fi + 1) * 128],
                                            hc[:, fi * T:(fi + 1) * T],
                                            start=(fi == 0), stop=False)
                        return last
                    p.add("tensor", ymm_a,
                          waits=[w2tok[(dc, 0)], (ve, hc_ve[31]),
                                 (ve, yev_ve.get(dc - 2, 0))], inc=pe)
                    ymmA_pe[dc] = pe.n

                    def ymm_b(e, dc=dc):
                        pd = PA[dc % 2]
                        for fi in range(32, NFC):
                            e.matmul(pd[:, 0:T],
                                     w2s[1][:, (fi - 32) * 128:(fi - 31) * 128],
                                     hc[:, fi * T:(fi + 1) * T],
                                     start=False, stop=False)
                        e.matmul(pd[:, 0:T], w2s[1][0:64, 4096:4224],
                                 zc2b[:], start=False, stop=False)
                        return e.matmul(pd[:, 0:T],
                                        b2r_sb[0:1, dc * 128:(dc + 1) * 128],
                                        rows[0:1, 3 * T:4 * T],
                                        start=False, stop=True)
                    p.add("tensor", ymm_b,
                          waits=[w2tok[(dc, 1)], (ve, zc2tok),
                                 (ve, hc_ve[NFC - 1]), (ldx, cstok)],
                          inc=pe)
                    ymmB_pe[dc] = pe.n
                    ytok = pe.n
                    p.add("vector",
                          lambda e, dc=dc: e.tensor_copy(
                              yfb[:, dc * T:(dc + 1) * T], PA[dc % 2][:, 0:T]),
                          waits=[(pe, ytok)], inc=ve)
                    yev_ve[dc] = ve.n
                    rb = dc % 2

                    def ysq(e, dc=dc, rb=rb):
                        e.copy(ybf[rb][:], yfb[:, dc * T:(dc + 1) * T])
                        return e.activation(y2b[rb][:], yfb[:, dc * T:(dc + 1) * T],
                                            AF.Square)
                    p.add("scalar", ysq,
                          waits=[(ve, yev_ve[dc]), (pe, yst_pe.get(dc - 2, 0))],
                          inc=se)
                    ysq_se[dc] = se.n
                    if dc >= 1:
                        emit_yst(dc - 1)
                emit_yst(NCH - 1)
                rowtok4, _ = ln_rows(p, srow, PB[0], PB[1], D, yst_pe[NCH - 1])
                prev_rowtok4 = rowtok4

                def bcast4(e):
                    e.matmul(PA[0][:, 0:T], onesr_f[:], srow[0:1, 0:T],
                             start=True, stop=True)
                    return e.matmul(PA[1][:, 0:T], onesr_f[:], srow[0:1, T:2 * T],
                                    start=True, stop=True)
                p.add("tensor", bcast4,
                      waits=[(se, rowtok4), (ve, yev_ve[NCH - 1])], inc=pe)
                bctok4 = pe.n

                def cpbc4(e):
                    e.tensor_copy(mbc[:], PA[0][:, 0:T])
                    return e.tensor_copy(rbc[:], PA[1][:, 0:T])
                p.add("vector", cpbc4, waits=[(pe, bctok4)], inc=ve)
                cptok4 = ve.n
                prev_cptok4 = cptok4
                o_ve = {}
                for dc in range(NCH):
                    rb = dc % 2
                    p.add("sync",
                          lambda e, dc=dc, rb=rb, t0=t0: e.dma_start(
                              out=x1c[rb][:], in_=x1_sp[dc, :, t0:t0 + T]),
                          waits=[(ve, o_ve.get(dc - 2, 0))], inc=ldx, k=16)
                    x1tok = ldx.n

                    def outf(e, dc=dc, rb=rb):
                        e.tensor_sub(outb[rb][:], yfb[:, dc * T:(dc + 1) * T], mbc[:])
                        e.tensor_mul(outb[rb][:], outb[rb][:], rbc[:])
                        return e.tensor_add(outb[rb][:], outb[rb][:], x1c[rb][:])
                    p.add("vector", outf,
                          waits=[(ldx, x1tok), (ve, cptok4),
                                 (stq, out_st.get((ti, dc - 2), 0))], inc=ve)
                    o_ve[dc] = ve.n
                    p.add("sync",
                          lambda e, dc=dc, rb=rb, t0=t0: e.dma_start(
                              out=out[dc * 128:(dc + 1) * 128, t0:t0 + T],
                              in_=outb[rb][:]),
                          waits=[(ve, o_ve[dc])], inc=stq, k=16)
                    out_st[(ti, dc)] = stq.n
                prev_o_ve = dict(o_ve)
                prev_ymm_end = ymmB_pe[NCH - 1]
            barrier(p)
            p.play(blk)
            ph.close()

    return nc


def _prep_inputs(ins):
    x = np.asarray(ins["L0_attn_x"], np.float32)
    Wq = np.asarray(ins["L0_attn_Wq"], np.float32)
    Wk = np.asarray(ins["L0_attn_Wk"], np.float32)
    Wv = np.asarray(ins["L0_attn_Wv"], np.float32)
    Wo = np.asarray(ins["L0_attn_Wo"], np.float32)
    W_r = np.asarray(ins["L0_ffn_W_r"], np.float32)
    b_r = np.asarray(ins["L0_ffn_b_r"], np.float32)
    W1 = np.asarray(ins["L0_ffn_W1"], np.float32)
    b1 = np.asarray(ins["L0_ffn_b1"], np.float32)
    W2 = np.asarray(ins["L0_ffn_W2"], np.float32)
    b2 = np.asarray(ins["L0_ffn_b2"], np.float32)
    A1 = np.asarray(ins["L0_ffn_A1"], np.float32)
    B1 = np.asarray(ins["L0_ffn_B1"], np.float32)
    A2 = np.asarray(ins["L0_ffn_A2"], np.float32)
    B2 = np.asarray(ins["L0_ffn_B2"], np.float32)

    def chunk_major(WT):
        din, o = WT.shape
        return np.ascontiguousarray(
            WT.reshape(din // 128, 128, o).transpose(1, 0, 2).reshape(128, -1))

    wqh = chunk_major(np.ascontiguousarray(Wq.T))
    wkh = chunk_major(np.ascontiguousarray(Wk.T))
    wvh = chunk_major(np.ascontiguousarray(Wv.T))
    woh = chunk_major(np.ascontiguousarray(Wo.T))
    wrh = chunk_major(np.ascontiguousarray(W_r.T))
    a1h = chunk_major(np.ascontiguousarray(A1.reshape(ER, D).T))

    W1T = np.ascontiguousarray(W1.T)
    w1base = W1T.reshape(NCH, 128, NFC, 128).transpose(2, 1, 0, 3).reshape(NFC, 128, D)
    b1n = B1.transpose(0, 2, 1).reshape(ER, FF)
    a2t = np.ascontiguousarray(A2.reshape(ER, FF).T)
    w1xh = np.zeros((NFC, 128, W1XC), np.float32)
    w1xh[:, :, :D] = w1base
    w1xh[:, :ER, D:D + 128] = b1n.reshape(ER, NFC, 128).transpose(1, 0, 2)
    w1xh[:, :, D + 128:D + 192] = a2t.reshape(NFC, 128, ER)

    W2T = np.ascontiguousarray(W2.T)
    w2base = W2T.reshape(NFC, 128, NCH, 128).transpose(2, 1, 0, 3).reshape(NCH, 128, FF)
    b2n = B2.transpose(0, 2, 1).reshape(ER, D)
    w2xh = np.zeros((NCH, 128, W2XC), np.float32)
    w2xh[:, :, :FF] = w2base
    w2xh[:, :ER, FF:FF + 128] = b2n.reshape(ER, NCH, 128).transpose(1, 0, 2)

    b1ch = np.ascontiguousarray(b1.reshape(NFC, 128).T)
    b2row = np.ascontiguousarray(b2.reshape(1, D))
    brcol = np.ascontiguousarray(b_r.reshape(E, 1))

    j = np.arange(0, DH, 2, dtype=np.float64) / DH
    inv = 10000.0 ** (-j)
    tpos = np.arange(S, dtype=np.float64)
    ang = inv[:, None] * tpos[None, :]
    cosa = np.cos(ang)
    sina = np.sin(ang)
    cosfh = np.empty((128, S), np.float32)
    sinfh = np.empty((128, S), np.float32)
    cosfh[0::2] = cosa
    cosfh[1::2] = cosa
    sinfh[0::2] = -sina
    sinfh[1::2] = sina

    kk = np.arange(128)[:, None]
    qq = np.arange(T)[None, :]

    def masks(par):
        m = np.zeros((128, 8, T), np.float32)
        for mi in range(8):
            if par == 0:
                if mi < 4:
                    m[:, mi, :] = (128 * mi + kk <= qq)
            else:
                if mi < 4:
                    m[:, mi, :] = 1.0
                else:
                    m[:, mi, :] = (128 * (mi - 4) + kk <= qq)
        return np.ascontiguousarray(m.reshape(128, 8 * T))

    x864h = np.zeros((E, ER), np.float32)
    for e_ in range(E):
        x864h[e_, e_ * R:(e_ + 1) * R] = 1.0

    in_maps, own_cols_all = [], []
    for c in range(8):
        b, par = c // 2, c % 2
        xfm = np.ascontiguousarray(x[b].T)
        own = np.concatenate(
            [np.arange(T) + T * (2 * s + par) for s in range(NSLOT)])
        own_cols_all.append(own)
        in_maps.append(dict(
            xf=xfm, xo=np.ascontiguousarray(xfm[:, own]),
            wq=wqh, wk=wkh, wv=wvh, wo=woh,
            w1x=w1xh, w2x=w2xh, a1r=a1h, wrr=wrh, brc=brcol, b1cd=b1ch,
            b2rowd=b2row, cosf=cosfh, sinf=sinfh,
            coso=np.ascontiguousarray(cosfh[:, own]),
            sino=np.ascontiguousarray(sinfh[:, own]),
            msk8=masks(par), x864d=x864h,
        ))
    return in_maps, own_cols_all


def kernel(**inputs):
    in_maps, own_cols = _prep_inputs(inputs)
    nc = build_program()
    res = run_bass_kernel_spmd(nc, in_maps, core_ids=list(range(8)))
    outp = np.empty((B, S, D), np.float32)
    for c in range(8):
        b = c // 2
        y = np.asarray(res.results[c]["out"])
        outp[b, own_cols[c], :] = y.T
    return outp
